# version 07
# baseline (speedup 1.0000x reference)
import math
import numpy as np

import concourse.bass as bass
import concourse.mybir as mybir
from concourse.bass_utils import run_bass_kernel_spmd

F32 = mybir.dt.float32
BF16 = mybir.dt.bfloat16
I32 = mybir.dt.int32
AF = mybir.ActivationFunctionType
ALU = mybir.AluOpType
AX = mybir.AxisListType

C = 1024
H = 16
NCH = 8
LN_EPS = 1e-5
WKV_EPS = 1e-6
K_CLAMP = 30.0
DEPTH = 6
INNER = 1024
DH = 64
WIN = 256
DI = 2730
D2 = 2 * DI
QK_SCALE = 8.0

TWO_PI = 2.0 * math.pi
_CW1 = np.float32(6.25)
_CW2 = np.float32(np.float64(TWO_PI) - 6.25)
_CW3 = np.float32(np.float64(TWO_PI) - np.float64(_CW1) - np.float64(_CW2))

ENGS = ("T", "V", "S", "G", "Y")
ATTN_PHASES = 9
ATTN_DBG = False


class PB:

    def __init__(self, nc):
        self.nc = nc
        self.th = {e: [] for e in ENGS}
        self.sem_h = {}
        self.cnt = {}
        self.maxw = {}
        self.last = {e: None for e in ENGS}

    def sem(self, name):
        if name not in self.sem_h:
            self.sem_h[name] = self.nc.alloc_semaphore(name)
            self.cnt[name] = 0
        return name

    def tok(self, name):
        return (name, self.cnt[name])

    def wait(self, eng, tok):
        if tok is None:
            return
        name, val = tok
        if name.startswith("d"):
            val = self.cnt[name]
        if val <= 0:
            return
        if not hasattr(self, "wmax"):
            self.wmax = {}
        self.wmax[name] = max(self.wmax.get(name, 0), val)
        key = (eng, name)
        if self.maxw.get(key, 0) >= val:
            return
        self.maxw[key] = val
        h = self.sem_h[name]
        self.th[eng].append(lambda E, h=h, v=val: E.wait_ge(h, v))

    def waits(self, eng, toks):
        for t in toks:
            self.wait(eng, t)

    def op(self, eng, fn, inc=None, n=1):
        if inc is None:
            self.th[eng].append(fn)
            return None
        name = self.sem(inc)
        if name.startswith("d"):
            wv = getattr(self, "wmax", {}).get(name, 0)
            if wv > 0:
                self.wait(eng, (name, wv))
        self.cnt[name] += n
        h = self.sem_h[name]
        self.th[eng].append(lambda E, fn=fn, h=h, n=n: fn(E).then_inc(h, n))
        t = (name, self.cnt[name])
        self.last[eng] = t
        return t

    def finish(self, block):
        block.sync(lambda E: [f(E) for f in self.th["Y"]])
        block.tensor(lambda E: [f(E) for f in self.th["T"]])
        block.vector(lambda E: [f(E) for f in self.th["V"]])
        block.scalar(lambda E: [f(E) for f in self.th["S"]])
        block.gpsimd(lambda E: [f(E) for f in self.th["G"]])


def ceil_div(a, b):
    return (a + b - 1) // b


def build(T=4096, tap=None, skip_pre=False):
    nc = bass.Bass("TRN2", target_bir_lowering=False)
    B = 2
    NT1 = T // 512
    NT2 = T // 256

    dram = {}

    def din(name, shape, dtype=F32):
        dram[name] = nc.dram_tensor(name, list(shape), dtype, kind="ExternalInput").ap()
        return dram[name]

    def dint(name, shape, dtype=F32):
        dram[name] = nc.dram_tensor(name, list(shape), dtype, kind="Internal").ap()
        return dram[name]

    x_in = din("x_fm", (B, C, T))
    wkT = din("wkT", (C, C)); wvT = din("wvT", (C, C))
    wrT = din("wrT", (C, C)); woT = din("woT", (C, C))
    wffkT = din("wffkT", (C, 4 * C)); wffvT = din("wffvT", (4 * C, C))
    ln1g = din("ln1g", (C,)); ln1b = din("ln1b", (C,))
    ln2g = din("ln2g", (C,)); ln2b = din("ln2b", (C,))
    td_in = din("td", (C,)); tf_in = din("tf", (C,))
    wqkv = din("wqkv", (DEPTH, C, 3 * INNER))
    wo2 = din("wo2", (DEPTH, INNER, C))
    w1 = din("w1", (DEPTH, C, D2))
    w2 = din("w2", (DEPTH, DI, C))
    b1_in = din("b1", (DEPTH, D2)); b2_in = din("b2", (DEPTH, C))
    lng_in = din("lng", (DEPTH, C)); lnb_in = din("lnb", (DEPTH, C))
    qs_in = din("qs", (DEPTH, DH)); ks_in = din("ks", (DEPTH, DH))

    scr1 = dint("scr1", (B, C, T))
    scrA = dint("scrA", (B, C, T))
    scrP = dint("scrP", (B, C, T))
    scrQ = dint("scrQ", (B, C, T))
    scr1b = dint("scr1b", (B, C, T), BF16)
    scrAb = dint("scrAb", (B, C, T), BF16)
    scrPb = dint("scrPb", (B, C, T), BF16)
    scrQb = dint("scrQb", (B, C, T), BF16)
    tabs = (dint("tabs", (4, 128, T), BF16) if not ATTN_DBG else
            nc.dram_tensor("tabs", [4, 128, T], BF16, kind="ExternalOutput").ap())

    out = nc.dram_tensor("out_fm", [B, C, T], F32, kind="ExternalOutput").ap()

    pb = PB(nc)

    A = lambda name, shape, dt=F32: nc.alloc_sbuf_tensor(name, list(shape), dt)
    ident_bf = A("ident_bf", [128, 128], BF16)
    r2_bf = A("r2_bf", [128, 128], BF16)
    onescol_f = A("onescol_f", [1, 128], F32)
    onescol_b = A("onescol_b", [1, 128], BF16)
    onessum_b = A("onessum_b", [128, 1], BF16)
    onesrow64 = A("onesrow64", [128, 128], BF16)
    hind = A("hind", [128, 8 * 16], BF16)
    ind16 = A("ind16", [16, 8 * 128], BF16)
    qs8_pp = A("qs8_pp", [128, DEPTH]); ksc_pp = A("ksc_pp", [128, DEPTH])
    iqs8_pp = A("iqs8_pp", [128, DEPTH]); iksc_pp = A("iksc_pp", [128, DEPTH])
    iqs8sq_pp = A("iqs8sq_pp", [128, DEPTH]); iksc2_pp = A("iksc2_pp", [128, DEPTH])
    epsln_pp = A("epsln_pp", [128, 1]); epsw_pp = A("epsw_pp", [128, 1])
    ones512_b = A("ones512_b", [1, 512], BF16)
    negC_pp = A("negC_pp", [128, DEPTH])
    decay_pp = A("decay_pp", [128, NCH]); tf_pp = A("tf_pp", [128, NCH])
    invf_pp = A("invf_pp", [128, 1]); lsc_pp = A("lsc_pp", [128, 1])

    ps6 = nc.alloc_psum_tensor("ps6", [128, 6 * 512], F32)
    psA = nc.alloc_psum_tensor("psA", [128, 512], F32)
    psB = nc.alloc_psum_tensor("psB", [128, 512], F32)
    def bank(i):
        return ps6[:, 512 * i:512 * (i + 1)]

    ARENA_BYTES = (nc.sbuf_bytes_remaining - 512) & ~127
    arena = nc.alloc_sbuf_tensor("arena", [128, ARENA_BYTES], mybir.dt.uint8)
    arena_base = nc.lookup_mloc(arena).addr
    _arena_names = {}

    def at(name, shape, dtype, off):
        key = (name, tuple(shape), dtype, off)
        if key not in _arena_names:
            dt_size = mybir.dt.size(dtype)
            nbytes = int(np.prod(shape[1:])) * dt_size
            assert off + nbytes <= ARENA_BYTES, f"{name}: {off}+{nbytes} > arena"
            _arena_names[key] = nc.alloc_sbuf_tensor_at(
                f"{name}_{off}", list(shape), dtype, offset=arena_base + off)
        return _arena_names[key]

    ii128 = at("ii128", [128, 128], I32, 0)
    itmp = at("itmp", [128, 128], F32, 512)
    teps1 = pb.op("V", lambda E: E.memset(epsln_pp[:, :], LN_EPS), inc="eV")
    tones512 = pb.op("V", lambda E: E.memset(ones512_b[:, :], 1.0), inc="eV")
    teps2 = pb.op("V", lambda E: E.memset(epsw_pp[:, :], WKV_EPS), inc="eV")
    tG = pb.op("G", lambda E: E.iota(ii128[:, :], [[1, 128]], base=0,
                                     channel_multiplier=-1), inc="eG")
    pb.wait("V", tG)
    tI = pb.op("V", lambda E: E.tensor_single_scalar(ident_bf[:, :], ii128[:, :], 0.0, ALU.is_equal), inc="eV")
    c1t = at("c1t", [128, 128], F32, 1024)
    c2t = at("c2t", [128, 128], F32, 1536)
    t1 = pb.op("V", lambda E: E.tensor_single_scalar(c1t[:, :], ii128[:, :], 32.0, ALU.is_equal), inc="eV")
    t2 = pb.op("V", lambda E: E.tensor_single_scalar(c2t[:, :], ii128[:, :], -32.0, ALU.is_equal), inc="eV")
    pb.wait("V", t2)
    for sl in ((0, 32), (64, 96)):
        pb.op("V", lambda E, sl=sl: E.memset(c1t[:, sl[0]:sl[1]], 0.0), inc="eV")
    for sl in ((32, 64), (96, 128)):
        pb.op("V", lambda E, sl=sl: E.memset(c2t[:, sl[0]:sl[1]], 0.0), inc="eV")
    pb.wait("V", pb.tok("eV"))
    tR = pb.op("V", lambda E: E.tensor_sub(r2_bf[:, :], c1t[:, :], c2t[:, :]), inc="eV")

    pb.op("V", lambda E: E.memset(onescol_f[:, :], 1.0), inc="eV")
    pb.op("V", lambda E: E.memset(onescol_b[:, :], 1.0), inc="eV")
    pb.op("V", lambda E: E.memset(onessum_b[:, :], 1.0), inc="eV")
    pb.op("V", lambda E: E.memset(onesrow64[:, :], 0.0), inc="eV")
    tor = pb.tok("eV")
    pb.wait("V", tor)
    pb.op("V", lambda E: E.memset(onesrow64[32:33, :], 1.0), inc="eV")
    pb.op("V", lambda E: E.memset(onesrow64[64:65, :], 1.0), inc="eV")
    pb.op("V", lambda E: E.memset(hind[:, :], 0.0), inc="eV")
    ii16 = at("ii16", [16, 1024], I32, 2048 + 128 * 1024)
    tg16 = pb.op("G", lambda E: E.iota(ii16[:, :].rearrange("p (g r) -> p g r", g=16), [[1, 16], [0, 64]], base=0, channel_multiplier=-1), inc="eG")
    pb.waits("V", [tg16])
    pb.op("V", lambda E: E.tensor_single_scalar(ind16[:, :], ii16[:, :], 0.0, ALU.is_equal), inc="eV")
    th0 = pb.tok("eV")
    pb.wait("V", th0)
    for c in range(NCH):
        pb.op("V", lambda E, c=c: E.memset(hind[0:64, 16 * c + 2 * c:16 * c + 2 * c + 1], 1.0), inc="eV")
        pb.op("V", lambda E, c=c: E.memset(hind[64:128, 16 * c + 2 * c + 1:16 * c + 2 * c + 2], 1.0), inc="eV")


    def pieces_for(k0):
        out_ = []
        for k in range(k0, 6):
            jbr = -256 + 128 * k
            s = max(jbr, 0)
            e = min(jbr + 384, 512)
            out_.append((k, jbr, s, e - s))
        return out_

    def load_pp(dst, src_ap, sem="dC"):
        return pb.op("G", lambda E: E.dma_start(out=dst, in_=src_ap), inc=sem, n=16)

    tdpp_l = []
    for c in range(NCH):
        tdpp_l.append(load_pp(decay_pp[:, c:c + 1], td_in[128 * c:128 * (c + 1)]))
        tdpp_l.append(load_pp(tf_pp[:, c:c + 1], tf_in[128 * c:128 * (c + 1)]))
    pb.waits("S", tdpp_l)
    t_s1 = pb.op("S", lambda E: E.activation(decay_pp[:, :], decay_pp[:, :], AF.Exp), inc="eS")
    pb.wait("S", t_s1)
    t_s2 = pb.op("S", lambda E: E.activation(decay_pp[:, :], decay_pp[:, :], AF.Exp, scale=-1.0), inc="eS")
    pb.wait("S", t_s2)
    t_s3 = pb.op("S", lambda E: E.activation(tf_pp[:, :], tf_pp[:, :], AF.Exp), inc="eS")

    qsrow = at("qsrow", [1, 128], F32, 2048)
    _qk_prev = None
    for l in range(DEPTH):
        pb.waits("G", [_qk_prev])
        l1 = load_pp(qs8_pp[0:64, l:l + 1], qs_in[l, :])
        l2 = load_pp(qs8_pp[64:128, l:l + 1], qs_in[l, :])
        l3 = load_pp(ksc_pp[0:64, l:l + 1], ks_in[l, :])
        l4 = load_pp(ksc_pp[64:128, l:l + 1], ks_in[l, :])
        l5 = load_pp(qsrow[0:1, 0:64], qs_in[l, None, :])
        l6 = load_pp(qsrow[0:1, 64:128], ks_in[l, None, :])
        pb.waits("V", [l5, l6])
        tqk = pb.op("V", lambda E: E.tensor_mul(qsrow[0:1, 0:64], qsrow[0:1, 0:64], qsrow[0:1, 64:128]), inc="eV")
        pb.wait("V", tqk)
        tmx = pb.op("V", lambda E: E.reduce_max(qsrow[0:1, 64:65], qsrow[0:1, 0:64], AX.X), inc="eV")
        pb.wait("V", tmx)
        tc = pb.op("V", lambda E: E.tensor_scalar_mul(qsrow[0:1, 65:66], qsrow[0:1, 64:65], -8.0), inc="eV")
        pb.waits("T", [tc])
        tb = pb.op("T", lambda E: E.matmul(psA[:, 0:1], onescol_f[0:1, :], qsrow[0:1, 65:66], start=True, stop=True), inc="eT")
        pb.waits("V", [tb])
        tv = pb.op("V", lambda E, l=l: E.tensor_copy(negC_pp[:, l:l + 1], psA[:, 0:1]), inc="eV")
        pb.wait("T", tv)
        pb.waits("V", [pb.last["V"]])
        tq8 = pb.op("V", lambda E, l=l: E.tensor_scalar_mul(qs8_pp[:, l:l + 1], qs8_pp[:, l:l + 1], QK_SCALE), inc="eV")
        pb.wait("V", tq8)
        pb.op("V", lambda E, l=l: E.reciprocal(iqs8_pp[:, l:l + 1], qs8_pp[:, l:l + 1]), inc="eV")
        pb.op("V", lambda E, l=l: E.reciprocal(iksc_pp[:, l:l + 1], ksc_pp[:, l:l + 1]), inc="eV")
        pb.waits("V", [pb.last["V"]])
        pb.op("V", lambda E, l=l: E.tensor_mul(iqs8sq_pp[:, l:l + 1], iqs8_pp[:, l:l + 1], iqs8_pp[:, l:l + 1]), inc="eV")
        pb.op("V", lambda E, l=l: E.tensor_mul(iksc2_pp[:, l:l + 1], iksc_pp[:, l:l + 1], iksc_pp[:, l:l + 1]), inc="eV")
        _qk_prev = pb.tok("eV")

    i32 = at("i32", [32, 1], I32, 2560)
    f32t = at("f32t", [32, 1], F32, 2816)
    tg2 = pb.op("G", lambda E: E.iota(i32[:, :], [[1, 1]], base=0, channel_multiplier=1), inc="eG")
    pb.waits("V", [tg2])
    tf32 = pb.op("V", lambda E: E.tensor_copy(f32t[:, :], i32[:, :]), inc="eV")
    pb.waits("V", [tf32])
    tiv = pb.op("V", lambda E: E.tensor_scalar_mul(invf_pp[0:32, :], f32t[:, :], -math.log(10000.0) / 32.0), inc="eV")
    tbs = pb.op("V", lambda E: E.tensor_scalar(lsc_pp[0:32, :], f32t[:, :], 2.0 / 89.6, 25.6 / 89.6, ALU.mult, ALU.add), inc="eV")
    pb.waits("S", [tiv, tbs])
    ts1 = pb.op("S", lambda E: E.activation(invf_pp[0:32, :], invf_pp[0:32, :], AF.Exp), inc="eS")
    ts2 = pb.op("S", lambda E: E.activation(lsc_pp[0:32, :], lsc_pp[0:32, :], AF.Ln), inc="eS")
    pb.waits("Y", [ts1, ts2])
    trep = []
    for bp in (32, 64, 96):
        trep.append(pb.op("Y", lambda E, bp=bp: E.dma_start(out=invf_pp[bp:bp + 32, :], in_=invf_pp[0:32, :]), inc="dT", n=16))
        trep.append(pb.op("Y", lambda E, bp=bp: E.dma_start(out=lsc_pp[bp:bp + 32, :], in_=lsc_pp[0:32, :]), inc="dT", n=16))

    trow = at("trow", [1, T], F32, 3072)
    trowi = at("trowi", [1, T], I32, 3072 + 4 * T)
    for tb0 in range(0, T, 1024):
        _tch = min(1024, T - tb0)
        tg3 = pb.op("G", lambda E, tb0=tb0, _tch=_tch: E.iota(trowi[:, tb0:tb0 + _tch], [[1, _tch]], base=tb0, channel_multiplier=0), inc="eG")
    pb.waits("V", [tg3])
    ttr = pb.op("V", lambda E: E.tensor_copy(trow[:, :], trowi[:, :]), inc="eV")

    TBB = 3072 + 8 * T
    bA = at("tb_A", [128, 512], F32, TBB)
    bB = at("tb_B", [128, 512], F32, TBB + 2048)
    bC_ = at("tb_C", [128, 512], I32, TBB + 4096)
    bD = at("tb_D", [128, 512], F32, TBB + 6144)
    bS1 = at("tb_S1", [128, 512], F32, TBB + 8192)
    bS2 = at("tb_S2", [128, 512], F32, TBB + 10240)
    bH = at("tb_H", [128, 512], F32, TBB + 12288)
    bI = at("tb_I", [128, 512], F32, TBB + 14336)
    bO = at("tb_O", [128, 512], BF16, TBB + 16384)
    pb.waits("T", [ttr] + trep)

    def sin_of(pb, theta_src, out_dst, toks):
        t1 = pb.op("V", lambda E: E.tensor_scalar_mul(bB[:, :], theta_src, 1.0 / TWO_PI), inc="eV")
        pb.wait("V", t1)
        t2 = pb.op("V", lambda E: E.tensor_copy(bC_[:, :], bB[:, :]), inc="eV")
        pb.wait("V", t2)
        t3 = pb.op("V", lambda E: E.tensor_copy(bB[:, :], bC_[:, :]), inc="eV")
        pb.wait("V", t3)
        t4 = None
        for cw in (float(_CW1), float(_CW2), float(_CW3)):
            src_ap = theta_src if t4 is None else out_dst
            t4 = pb.op("V", lambda E, cw=cw, src_ap=src_ap: E.scalar_tensor_tensor(
                out_dst, bB[:, :], -cw, src_ap, ALU.mult, ALU.add), inc="eV")
            pb.wait("V", t4)
        t5a = pb.op("V", lambda E: E.tensor_single_scalar(bB[:, :], out_dst, math.pi, ALU.is_ge), inc="eV")
        pb.wait("V", t5a)
        t5b = pb.op("V", lambda E: E.scalar_tensor_tensor(out_dst, bB[:, :], -TWO_PI, out_dst, ALU.mult, ALU.add), inc="eV")
        pb.wait("V", t5b)
        pb.waits("S", [t5b])
        t6 = pb.op("S", lambda E: E.activation(out_dst, out_dst, AF.Sin), inc="eS")
        return t6

    for cb in range(T // 512):
        c0 = 512 * cb
        tp = pb.op("T", lambda E, c0=c0: E.matmul(psB[:, :], onescol_f[0:1, :], trow[0:1, c0:c0 + 512], start=True, stop=True), inc="eT")
        pb.waits("V", [tp])
        ta = pb.op("V", lambda E: E.tensor_scalar(bA[:, :], psB[:, :], invf_pp[:, 0:1], None, ALU.mult), inc="eV")
        tpw = pb.op("V", lambda E: E.tensor_scalar(bH[:, :], psB[:, :], -float(T // 2), 1.0 / 128.0, ALU.add, ALU.mult), inc="eV")
        pb.wait("V", tpw)
        tpw2 = pb.op("V", lambda E: E.tensor_scalar(bH[:, :], bH[:, :], lsc_pp[:, 0:1], None, ALU.mult), inc="eV")
        pb.wait("T", ta)
        pb.wait("T", tpw)
        pb.wait("V", ta)
        ts_sin = sin_of(pb, bA[:, :], bS1[:, :], [])
        pb.waits("V", [ts_sin])
        tth2 = pb.op("V", lambda E: E.tensor_scalar_add(bA[:, :], bA[:, :], math.pi / 2.0), inc="eV")
        pb.wait("V", tth2)
        ts_cos = sin_of(pb, bA[:, :], bS2[:, :], [])
        pb.waits("S", [tpw2])
        tsc = pb.op("S", lambda E: E.activation(bI[:, :], bH[:, :], AF.Exp), inc="eS")
        pb.wait("S", tsc)
        tisc = pb.op("S", lambda E: E.activation(bH[:, :], bH[:, :], AF.Exp, scale=-1.0), inc="eS")
        combos = [(0, bS2, bI, tsc, ts_cos), (1, bS1, bI, tsc, ts_sin),
                  (2, bS2, bH, tisc, ts_cos), (3, bS1, bH, tisc, ts_sin)]
        for (ti, s_t, sc_t, tk1, tk2) in combos:
            pb.waits("V", [tk1, tk2])
            tv = pb.op("V", lambda E, s_t=s_t, sc_t=sc_t: E.scalar_tensor_tensor(
                bO[:, :], s_t[:, :], 1.0, sc_t[:, :], ALU.mult, ALU.mult), inc="eV")
            pb.waits("Y", [tv])
            tw = pb.op("Y", lambda E, ti=ti, c0=c0: E.dma_start(out=tabs[ti, :, c0:c0 + 512], in_=bO[:, :]), inc="dT", n=16)
            pb.wait("V", tw)

    prologue_toks = [pb.last[e] for e in ("T", "V", "S", "G")] + [pb.tok("dT"), pb.tok("dC")]

    s1_wk = at("s1_wk", [128, NCH * C], BF16, 0)
    s1_wv = at("s1_wv", [128, NCH * C], BF16, 16 * 1024)
    s1_wr = at("s1_wr", [128, NCH * C], BF16, 32 * 1024)
    s1_wo = at("s1_wo", [128, NCH * C], BF16, 48 * 1024)
    s1_x = at("s1_x", [128, NCH * 512], F32, 64 * 1024)
    s1_xb = at("s1_xb", [128, NCH * 512], BF16, 80 * 1024)
    s1_xl = at("s1_xl", [128, NCH * 512], BF16, 88 * 1024)
    s1_w = at("s1_w", [128, NCH * 512], F32, 96 * 1024)
    s1_wv2 = at("s1_wv2", [128, NCH * 512], F32, 112 * 1024)
    s1_sr = at("s1_sr", [128, NCH * 512], BF16, 128 * 1024)
    s1_rk = at("s1_rk", [128, NCH * 512], BF16, 136 * 1024)
    s1_dec = at("s1_dec", [128, NCH * 512], F32, 144 * 1024)
    s1_x2 = at("s1_x2", [128, 512], BF16, 160 * 1024)
    s1_car = at("s1_car", [128, 2 * NCH], F32, 161 * 1024)
    s1_ms = at("s1_ms", [1, 512 * 4], F32, 163 * 1024)
    s1_x1 = at("s1_x1", [128, NCH * 512], F32, 172 * 1024)
    gb_rw1 = at("gb_rw1", [2, C], BF16, 188 * 1024)
    g1pp = at("g1pp", [128, NCH], F32, 190 * 1024)
    b1pp_rw = at("b1pp_rw", [128, NCH], F32, 190 * 1024 + 64)

    pb.waits("G", prologue_toks)
    wload = []
    for c in range(NCH):
        wload.append(load_pp(g1pp[:, c:c + 1], ln1g[128 * c:128 * (c + 1)]))
        wload.append(load_pp(b1pp_rw[:, c:c + 1], ln1b[128 * c:128 * (c + 1)]))
    wload.append(load_pp(gb_rw1[0:1, :], ln1g[None, :]))
    wload.append(load_pp(gb_rw1[1:2, :], ln1b[None, :]))
    for (dst, src) in ((s1_wk, wkT), (s1_wv, wvT), (s1_wr, wrT), (s1_wo, woT)):
        for c in range(NCH):
            wload.append(pb.op("G", lambda E, dst=dst, src=src, c=c: E.dma_start(
                out=dst[:, C * c:C * (c + 1)], in_=src[128 * c:128 * (c + 1), :]), inc="dW", n=16))
    tmem = pb.op("V", lambda E: E.memset(s1_dec[:, :], 1.0), inc="eV")
    pb.wait("V", tmem)
    for c in range(NCH):
        pb.op("V", lambda E, c=c: E.tensor_scalar(
            s1_dec[:, 512 * c:512 * (c + 1)], s1_dec[:, 512 * c:512 * (c + 1)],
            decay_pp[:, c:c + 1], None, ALU.mult), inc="eV")
    tdec = pb.tok("eV")

    def ln_stats_apply(xf, xbf, xlout, gb2ap, gpp_ap, dep_toks, sq, msr, scr,
                       width=512, fp32_bcast=False, no_gb=False, outf32=None,
                       bpp_ap=None, sq_eng="G"):
        pb.waits("T", dep_toks)
        tk = None
        for c in range(NCH):
            tk = pb.op("T", lambda E, c=c: E.matmul(
                psA[0:1, 0:width], onessum_b[:, 0:1], xbf[:, width * c:width * c + width],
                start=(c == 0), stop=(c == NCH - 1)),
                inc=("eT" if c == NCH - 1 else None))
        t_sum = tk
        ginc = "eG" if sq_eng == "G" else "eS"
        pb.waits(sq_eng, dep_toks)
        if xlout is not None:
            slot_of = lambda c: xlout[:, width * c:width * c + width]
            sq_toks = []
            for c in range(NCH):
                tq = pb.op(sq_eng, lambda E, c=c: E.tensor_mul(
                    slot_of(c), xbf[:, width * c:width * c + width],
                    xbf[:, width * c:width * c + width]), inc=ginc)
                sq_toks.append(tq)
            t_sq_mm = None
            for c in range(NCH):
                pb.waits("T", [sq_toks[c]])
                t_sq_mm = pb.op("T", lambda E, c=c: E.matmul(
                    psB[0:1, 0:width], onessum_b[:, 0:1], slot_of(c),
                    start=(c == 0), stop=(c == NCH - 1)), inc="eT")
        else:
            t_sq_mm = None
            for c in range(NCH):
                pb.waits(sq_eng, [t_sq_mm])
                tq = pb.op(sq_eng, lambda E, c=c: E.tensor_mul(
                    sq[:, 0:width], xbf[:, width * c:width * c + width],
                    xbf[:, width * c:width * c + width]), inc=ginc)
                pb.waits("T", [tq])
                t_sq_mm = pb.op("T", lambda E, c=c: E.matmul(
                    psB[0:1, 0:width], onessum_b[:, 0:1], sq[:, 0:width],
                    start=(c == 0), stop=(c == NCH - 1)), inc="eT")
        pb.waits("V", [t_sum, t_sq_mm])
        tm = pb.op("V", lambda E: E.tensor_scalar_mul(msr[0:1, 0:width], psA[0:1, 0:width], 1.0 / C), inc="eV")
        pb.wait("V", tm)
        tms = pb.op("V", lambda E: E.tensor_mul(msr[0:1, width:2 * width], msr[0:1, 0:width], msr[0:1, 0:width]), inc="eV")
        pb.wait("V", tms)
        tvr = pb.op("V", lambda E: E.scalar_tensor_tensor(
            msr[0:1, width:2 * width], psB[0:1, 0:width], 1.0 / C, msr[0:1, width:2 * width],
            ALU.mult, ALU.subtract), inc="eV")
        pb.waits("S", [tvr])
        tln = pb.op("S", lambda E: E.activation(msr[0:1, 2 * width:3 * width], msr[0:1, width:2 * width], AF.Ln, bias=epsln_pp[0:1, 0:1]), inc="eS")
        pb.wait("S", tln)
        trs = pb.op("S", lambda E: E.activation(msr[0:1, 2 * width:3 * width], msr[0:1, 2 * width:3 * width], AF.Exp, scale=-0.5), inc="eS")
        if fp32_bcast:
            pb.waits("T", [trs, tvr, tm])
            t_rb = pb.op("T", lambda E: E.matmul(psA[:, 0:width], onescol_f[0:1, :], msr[0:1, 2 * width:3 * width], start=True, stop=True), inc="eT")
            t_mb = pb.op("T", lambda E: E.matmul(psB[:, 0:width], onescol_f[0:1, :], msr[0:1, 0:width], start=True, stop=True), inc="eT")
        else:
            rstd_row, cb_rows = scr
            pb.waits("V", [trs])
            trsb = pb.op("V", lambda E: E.tensor_copy(rstd_row[0:1, 0:width], msr[0:1, 2 * width:3 * width]), inc="eV")
            tmb_ = pb.op("V", lambda E: E.tensor_copy(cb_rows[0:1, 0:width], msr[0:1, 0:width]), inc="eV")
            pb.waits("T", [trsb, tmb_, tvr, tm])
            t_rb = pb.op("T", lambda E: E.matmul(psA[:, 0:width], onescol_b[0:1, :], rstd_row[0:1, 0:width], start=True, stop=True), inc="eT")
            t_mb = pb.op("T", lambda E: E.matmul(psB[:, 0:width], onescol_b[0:1, :], cb_rows[0:1, 0:width], start=True, stop=True), inc="eT")
        toks_c = []
        for c in range(NCH):
            dst = xlout if not no_gb else (outf32 if outf32 is not None else xlout)
            pb.waits("V", [t_rb, t_mb])
            tv1 = pb.op("V", lambda E, c=c, dst=dst: E.tensor_tensor(
                dst[:, width * c:width * c + width], xf[:, width * c:width * c + width],
                psB[:, 0:width], ALU.subtract), inc="eV")
            pb.wait("V", tv1)
            if no_gb:
                tv2 = pb.op("V", lambda E, c=c, dst=dst: E.tensor_tensor(
                    dst[:, width * c:width * c + width], dst[:, width * c:width * c + width],
                    psA[:, 0:width], ALU.mult), inc="eV")
                toks_c.append(tv2)
                if outf32 is not None and xlout is not None:
                    pb.wait("V", tv2)
                    tv3 = pb.op("V", lambda E, c=c: E.tensor_copy(
                        xlout[:, width * c:width * c + width], outf32[:, width * c:width * c + width]), inc="eV")
                    toks_c.append(tv3)
            else:
                tv2 = pb.op("V", lambda E, c=c, dst=dst: E.scalar_tensor_tensor(
                    dst[:, width * c:width * c + width], dst[:, width * c:width * c + width],
                    gpp_ap[:, c:c + 1], psA[:, 0:width], ALU.mult, ALU.mult), inc="eV")
                pb.wait("V", tv2)
                tv3 = pb.op("V", lambda E, c=c, dst=dst: E.tensor_scalar(
                    dst[:, width * c:width * c + width], dst[:, width * c:width * c + width],
                    bpp_ap[:, c:c + 1], None, ALU.add), inc="eV")
                toks_c.append(tv3)
        return toks_c, t_rb

    if skip_pre:
        globals()
    st_s1 = []
    prev_x1_cons = {}
    s1_stb_prev = {}
    pb.waits("V", prologue_toks)
    pb.waits("S", prologue_toks)
    pb.waits("T", prologue_toks)
    last_store = []
    s1_rstd = at("s1_rstd", [1, 512], BF16, 162 * 1024)
    s1_cbr = at("s1_cbr", [2, 512], BF16, 162 * 1024 + 1024 + 128)
    s1_x1b = at("s1_x1b", [128, 2 * 512], BF16, 191 * 1024)
    pb.waits("Y", [tones512])
    ln_scr_prep = pb.op("Y", lambda E: E.dma_start(out=s1_cbr[1:2, :], in_=ones512_b[0:1, :]), inc="dO1", n=16)

    for b in range(B if not skip_pre else 0):
        for t in range(NT1):
            t0 = 512 * t
            lx = []
            lxb = []
            for c in range(NCH):
                pb.waits("Y", prev_x1_cons.get(("x", c), []))
                lx.append(pb.op("Y", lambda E, b=b, c=c, t0=t0: E.dma_start(
                    out=s1_x[:, 512 * c:512 * (c + 1)], in_=x_in[b, 128 * c:128 * (c + 1), t0:t0 + 512]), inc="dX", n=16))
                pb.waits("G", prev_x1_cons.get(("xb", c), []))
                lxb.append(pb.op("G", lambda E, b=b, c=c, t0=t0: E.dma_start(
                    out=s1_xb[:, 512 * c:512 * (c + 1)], in_=x_in[b, 128 * c:128 * (c + 1), t0:t0 + 512]), inc="dXb", n=16))
            pb.waits("S", lxb)
            pb.waits("V", lx + lxb + [ln_scr_prep] + wload)
            toks_xl, t_rb = ln_stats_apply(
                s1_x, s1_xb, s1_xl, gb_rw1, g1pp, lxb, s1_x2, s1_ms, (s1_rstd, s1_cbr),
                bpp_ap=b1pp_rw)
            prev_x1_cons[("xb", 0)] = [pb.tok("eT")]
            wkv_toks = []
            sr_toks = []
            for c in range(NCH):
                pb.waits("T", toks_xl + wload)
                for (wsb, bk) in ((s1_wk, 0), (s1_wv, 1), (s1_wr, 2)):
                    for ci in range(NCH):
                        tk = pb.op("T", lambda E, wsb=wsb, bk=bk, c=c, ci=ci: E.matmul(
                            bank(bk), wsb[:, C * ci + 128 * c:C * ci + 128 * (c + 1)],
                            s1_xl[:, 512 * ci:512 * (ci + 1)],
                            start=(ci == 0), stop=(ci == NCH - 1)),
                            inc=("eT" if ci == NCH - 1 else None))
                    if wsb is s1_wk:
                        t_k = tk
                    elif wsb is s1_wv:
                        t_v = tk
                    else:
                        t_r = tk
                pb.waits("V", [t_k])
                tw1 = pb.op("V", lambda E, c=c: E.tensor_scalar_min(
                    s1_w[:, 512 * c:512 * (c + 1)], bank(0), K_CLAMP), inc="eV")
                pb.waits("S", [tw1])
                tw2 = pb.op("S", lambda E, c=c: E.activation(
                    s1_w[:, 512 * c:512 * (c + 1)], s1_w[:, 512 * c:512 * (c + 1)], AF.Exp), inc="eS")
                pb.waits("V", [t_v, tw2])
                tw3 = pb.op("V", lambda E, c=c: E.tensor_tensor(
                    s1_wv2[:, 512 * c:512 * (c + 1)], s1_w[:, 512 * c:512 * (c + 1)], bank(1), ALU.mult), inc="eV")
                pb.waits("S", [t_r])
                tw4 = pb.op("S", lambda E, c=c: E.activation(
                    s1_sr[:, 512 * c:512 * (c + 1)], bank(2), AF.Sigmoid), inc="eS")
                sr_toks.append(tw4)
                pb.waits("T", [tw1, tw3, tw4])
                wkv_toks.append((c, tw2, tw3))
            scan_toks = []
            for (c, tw2, tw3) in wkv_toks:
                if t == 0:
                    pb.waits("V", [tw2])
                    ta1 = pb.op("V", lambda E, c=c: E.tensor_scalar(
                        s1_w[:, 512 * c:512 * c + 1], s1_w[:, 512 * c:512 * c + 1],
                        tf_pp[:, c:c + 1], None, ALU.mult), inc="eV")
                    ta2 = pb.op("V", lambda E, c=c: E.tensor_scalar(
                        s1_wv2[:, 512 * c:512 * c + 1], s1_wv2[:, 512 * c:512 * c + 1],
                        tf_pp[:, c:c + 1], None, ALU.mult), inc="eV")
                    tz = pb.op("V", lambda E, c=c: E.memset(s1_car[:, 2 * c:2 * c + 2], 0.0), inc="eV")
                    pb.waits("V", [ta1, ta2, tz])
                pb.waits("V", [tw2, tw3, tdec])
                ts1_ = pb.op("V", lambda E, c=c: E.tensor_tensor_scan(
                    s1_wv2[:, 512 * c:512 * (c + 1)], s1_dec[:, 512 * c:512 * (c + 1)],
                    s1_wv2[:, 512 * c:512 * (c + 1)], s1_car[:, 2 * c:2 * c + 1], ALU.mult, ALU.add), inc="eV")
                pb.wait("V", ts1_)
                ts2_ = pb.op("V", lambda E, c=c: E.tensor_tensor_scan(
                    s1_w[:, 512 * c:512 * (c + 1)], s1_dec[:, 512 * c:512 * (c + 1)],
                    s1_w[:, 512 * c:512 * (c + 1)], s1_car[:, 2 * c + 1:2 * c + 2], ALU.mult, ALU.add), inc="eV")
                pb.wait("V", ts2_)
                tc1 = pb.op("V", lambda E, c=c: E.tensor_copy(s1_car[:, 2 * c:2 * c + 1], s1_wv2[:, 512 * c + 511:512 * c + 512]), inc="eV")
                tc2 = pb.op("V", lambda E, c=c: E.tensor_copy(s1_car[:, 2 * c + 1:2 * c + 2], s1_w[:, 512 * c + 511:512 * c + 512]), inc="eV")
                pb.waits("V", [tc1, tc2])
                pb.waits("S", [ts2_, tc2])
                td1 = pb.op("S", lambda E, c=c: E.activation(
                    s1_w[:, 512 * c:512 * (c + 1)], s1_w[:, 512 * c:512 * (c + 1)], AF.Ln, bias=epsw_pp[:, 0:1]), inc="eS")
                pb.wait("S", td1)
                td2 = pb.op("S", lambda E, c=c: E.activation(
                    s1_w[:, 512 * c:512 * (c + 1)], s1_w[:, 512 * c:512 * (c + 1)], AF.Exp, scale=-1.0), inc="eS")
                pb.waits("V", [td2, tc1])
                td3 = pb.op("V", lambda E, c=c: E.tensor_tensor(
                    s1_wv2[:, 512 * c:512 * (c + 1)], s1_wv2[:, 512 * c:512 * (c + 1)],
                    s1_w[:, 512 * c:512 * (c + 1)], ALU.mult), inc="eV")
                pb.wait("V", td3)
                td4 = pb.op("V", lambda E, c=c: E.tensor_tensor(
                    s1_rk[:, 512 * c:512 * (c + 1)], s1_sr[:, 512 * c:512 * (c + 1)],
                    s1_wv2[:, 512 * c:512 * (c + 1)], ALU.mult), inc="eV")
                scan_toks.append(td4)
            pb.waits("T", scan_toks)
            pb.waits("T", last_store)
            st_tile = []
            for c in range(NCH):
                for ci in range(NCH):
                    tk = pb.op("T", lambda E, c=c, ci=ci: E.matmul(
                        bank(3), s1_wo[:, C * ci + 128 * c:C * ci + 128 * (c + 1)],
                        s1_rk[:, 512 * ci:512 * (ci + 1)],
                        start=(ci == 0), stop=(ci == NCH - 1)),
                        inc=("eT" if ci == NCH - 1 else None))
                pb.waits("V", [tk])
                te = pb.op("V", lambda E, c=c: E.scalar_tensor_tensor(
                    s1_x1[:, 512 * c:512 * (c + 1)], bank(3), 1.0,
                    s1_x[:, 512 * c:512 * (c + 1)], ALU.mult, ALU.add), inc="eV")
                pb.waits("V", [s1_stb_prev.get(c % 2), te])
                tcb = pb.op("V", lambda E, c=c: E.tensor_copy(
                    s1_x1b[:, 512 * (c % 2):512 * (c % 2) + 512],
                    s1_x1[:, 512 * c:512 * (c + 1)]), inc="eV")
                pb.waits("T", [te])
                pb.waits("Y", [te, tcb])
                ts_ = pb.op("Y", lambda E, b=b, c=c, t0=t0: E.dma_start(
                    out=scr1[b, 128 * c:128 * (c + 1), t0:t0 + 512],
                    in_=s1_x1[:, 512 * c:512 * (c + 1)]), inc="dS1", n=16)
                tsb_ = pb.op("Y", lambda E, b=b, c=c, t0=t0: E.dma_start(
                    out=scr1b[b, 128 * c:128 * (c + 1), t0:t0 + 512],
                    in_=s1_x1b[:, 512 * (c % 2):512 * (c % 2) + 512]), inc="dS1", n=16)
                s1_stb_prev[c % 2] = tsb_
                st_tile.append(ts_)
                st_tile.append(tsb_)
            last_store = st_tile
            st_s1.extend(st_tile)
            prev_x1_cons[("x", 0)] = [pb.tok("eV")]
            prev_x1_cons[("xb", 0)] = [pb.tok("eT"), pb.tok("eS")]
            for c in range(NCH):
                prev_x1_cons[("x", c)] = prev_x1_cons[("x", 0)]
                prev_x1_cons[("xb", c)] = prev_x1_cons[("xb", 0)]

    def stage_barrier_toks():
        return ([pb.last[e] for e in ("T", "V", "S", "G") if pb.last[e] is not None]
                + [pb.tok(s) for s in list(pb.cnt) if s.startswith("d")])

    def emit_tap(scr):
        tb_buf = at("tapbuf", [128, 512], F32, 0)
        for e in ENGS:
            pb.waits(e, stage_barrier_toks())
        prev = []
        for b in range(B):
            for t in range(NT1):
                for c in range(NCH):
                    pb.waits("Y", prev)
                    l1 = pb.op("Y", lambda E, b=b, c=c, t=t: E.dma_start(
                        out=tb_buf[:, :], in_=scr[b, 128 * c:128 * (c + 1), 512 * t:512 * (t + 1)]), inc="dTap", n=16)
                    pb.waits("Y", [l1])
                    s1_ = pb.op("Y", lambda E, b=b, c=c, t=t: E.dma_start(
                        out=out[b, 128 * c:128 * (c + 1), 512 * t:512 * (t + 1)], in_=tb_buf[:, :]), inc="dOut", n=16)
                    prev = [s1_]
        pb.op("Y", lambda E: E.wait_ge(pb.sem_h["dOut"], pb.cnt["dOut"]))
        with nc.Block() as block:
            pb.finish(block)
        return nc

    if tap == "scr1":
        return emit_tap(scr1)

    W2T = 256
    s2_wk = at("s2_wk", [128, NCH * 4096], BF16, 0)
    s2_wv = at("s2_wv", [128, 32 * 1024], BF16, 64 * 1024)
    s2_x1 = at("s2_x1", [128, NCH * W2T], F32, 128 * 1024)
    s2_x1b = at("s2_x1b", [128, NCH * W2T], BF16, 136 * 1024)
    s2_xl2 = at("s2_xl2", [128, NCH * W2T], BF16, 140 * 1024)
    s2_h = at("s2_h", [128, 32 * W2T], BF16, 144 * 1024)
    s2_x = at("s2_x", [128, NCH * W2T], F32, 160 * 1024)
    s2_s = at("s2_s", [128, NCH * W2T], F32, 168 * 1024)
    s2_sb = at("s2_sb", [128, NCH * W2T], BF16, 176 * 1024)
    s2_sq = at("s2_sq", [128, W2T], BF16, 180 * 1024)
    s2_ms = at("s2_ms", [1, 4 * W2T], F32, 181 * 1024)
    s2_rstd = at("s2_rstd", [1, W2T], BF16, 185 * 1024)
    s2_cbr = at("s2_cbr", [2, W2T], BF16, 185 * 1024 + 640)
    s2_xa = at("s2_xa", [128, NCH * W2T], F32, 187 * 1024)
    gb_rw2 = at("gb_rw2", [2, C], BF16, 196 * 1024)
    g2pp = at("g2pp", [128, NCH], F32, 198 * 1024 + 128)
    b2pp_rw = at("b2pp_rw", [128, NCH], F32, 198 * 1024 + 256)

    s1_done = stage_barrier_toks()
    pb.waits("G", s1_done)
    pb.waits("S", s1_done)
    w2load = []
    for ci in range(NCH):
        w2load.append(pb.op("G", lambda E, ci=ci: E.dma_start(
            out=s2_wk[:, 4096 * ci:4096 * (ci + 1)], in_=wffkT[128 * ci:128 * (ci + 1), :]), inc="dW", n=16))
    for j in range(32):
        w2load.append(pb.op("G", lambda E, j=j: E.dma_start(
            out=s2_wv[:, 1024 * j:1024 * (j + 1)], in_=wffvT[128 * j:128 * (j + 1), :]), inc="dW", n=16))
    for c in range(NCH):
        w2load.append(load_pp(g2pp[:, c:c + 1], ln2g[128 * c:128 * (c + 1)]))
        w2load.append(load_pp(b2pp_rw[:, c:c + 1], ln2b[128 * c:128 * (c + 1)]))
    w2load.append(load_pp(gb_rw2[0:1, :], ln2g[None, :]))
    w2load.append(load_pp(gb_rw2[1:2, :], ln2b[None, :]))
    pb.waits("Y", s1_done + [tones512])
    tcb2 = pb.op("Y", lambda E: E.dma_start(out=s2_cbr[1:2, :], in_=ones512_b[0:1, 0:W2T]), inc="dO1", n=16)
    w2load.append(tcb2)

    st_s2 = []
    s2_prev_cons = []
    s2_xl2_prev = []
    s2_psum_prev = []
    s2_stb_prev = []
    for b in range(B if not skip_pre else 0):
        for t in range(NT2):
            t0 = W2T * t
            lx = []
            pb.waits("Y", s2_prev_cons)
            pb.waits("G", s2_prev_cons + s2_xl2_prev)
            pb.waits("S", s2_prev_cons)
            for c in range(NCH):
                lx.append(pb.op("Y", lambda E, b=b, c=c, t0=t0: E.dma_start(
                    out=s2_x1[:, W2T * c:W2T * (c + 1)], in_=scr1[b, 128 * c:128 * (c + 1), t0:t0 + W2T]), inc="dX", n=16))
                lx.append(pb.op("S", lambda E, b=b, c=c, t0=t0: E.dma_start(
                    out=s2_x1b[:, W2T * c:W2T * (c + 1)], in_=scr1b[b, 128 * c:128 * (c + 1), t0:t0 + W2T]), inc="dXc", n=16))
                lx.append(pb.op("Y", lambda E, b=b, c=c, t0=t0: E.dma_start(
                    out=s2_x[:, W2T * c:W2T * (c + 1)], in_=x_in[b, 128 * c:128 * (c + 1), t0:t0 + W2T]), inc="dX", n=16))
            pb.waits("V", lx + w2load)
            pb.waits("S", lx)
            pb.waits("T", lx + w2load)
            toks_xl2, t_rb_ln2 = ln_stats_apply(
                s2_x1, s2_x1b, s2_xl2, gb_rw2, g2pp, lx + s2_psum_prev, s2_sq, s2_ms,
                (s2_rstd, s2_cbr), width=W2T, bpp_ap=b2pp_rw)
            tg_ln2 = pb.tok("eG")
            txx = []
            for c in range(NCH):
                txx.append(pb.op("V", lambda E, c=c: E.tensor_add(
                    s2_s[:, W2T * c:W2T * (c + 1)], s2_x[:, W2T * c:W2T * (c + 1)],
                    s2_x1[:, W2T * c:W2T * (c + 1)]), inc="eV"))
            s2_prev_cons = toks_xl2 + txx + [t_rb_ln2, tg_ln2]
            h_toks = []
            pb.waits("T", toks_xl2)
            s2_xl2_gate = []
            for j in range(32):
                bk = j % 4
                for ci in range(NCH):
                    tk = pb.op("T", lambda E, j=j, ci=ci, bk=bk: E.matmul(
                        bank(bk)[:, 0:W2T], s2_wk[:, 4096 * ci + 128 * j:4096 * ci + 128 * (j + 1)],
                        s2_xl2[:, W2T * ci:W2T * (ci + 1)],
                        start=(ci == 0), stop=(ci == NCH - 1)),
                        inc=("eT" if ci == NCH - 1 else None))
                pb.waits("S", [tk])
                tr = pb.op("S", lambda E, j=j, bk=bk: E.activation(
                    s2_h[:, W2T * j:W2T * (j + 1)], bank(bk)[:, 0:W2T], AF.Relu), inc="eS")
                pb.waits("V", [tr])
                tq = pb.op("V", lambda E, j=j: E.tensor_tensor(
                    s2_h[:, W2T * j:W2T * (j + 1)], s2_h[:, W2T * j:W2T * (j + 1)],
                    s2_h[:, W2T * j:W2T * (j + 1)], ALU.mult), inc="eV")
                h_toks.append(tq)
                s2_xl2_gate = [tk]
                pb.waits("T", [tr])
            pb.waits("T", h_toks)
            pb.waits("V", s2_stb_prev)
            s_toks = []
            for c in range(NCH):
                bk = 4 + (c % 2)
                for j in range(32):
                    tk = pb.op("T", lambda E, c=c, j=j, bk=bk: E.matmul(
                        bank(bk)[:, 0:W2T], s2_wv[:, 1024 * j + 128 * c:1024 * j + 128 * (c + 1)],
                        s2_h[:, W2T * j:W2T * (j + 1)],
                        start=(j == 0), stop=(j == 31)),
                        inc=("eT" if j == 31 else None))
                pb.waits("V", [tk] + txx)
                te = pb.op("V", lambda E, c=c, bk=bk: E.scalar_tensor_tensor(
                    s2_s[:, W2T * c:W2T * (c + 1)], bank(bk)[:, 0:W2T], 1.0,
                    s2_s[:, W2T * c:W2T * (c + 1)], ALU.mult, ALU.add), inc="eV")
                s_toks.append(te)
                pb.waits("T", [te])
                pb.wait("V", te)
                tcst = pb.op("V", lambda E, c=c: E.tensor_copy(
                    s2_sb[:, W2T * c:W2T * (c + 1)], s2_s[:, W2T * c:W2T * (c + 1)]), inc="eV")
                s_toks.append(tcst)
            pb.waits("S", s_toks)
            pb.waits("V", s_toks)
            toks_xa, t_rb_xa = ln_stats_apply(
                s2_s, s2_sb, None, None, None, s_toks, s2_sq, s2_ms,
                None, width=W2T, fp32_bcast=True, no_gb=True, outf32=s2_xa)
            st_tile = []
            pb.waits("V", [t_rb_xa, pb.tok("eG")] + toks_xa)
            tcbs = []
            for c in range(NCH):
                tcbs.append(pb.op("V", lambda E, c=c: E.tensor_copy(
                    s2_sb[:, W2T * c:W2T * (c + 1)], s2_xa[:, W2T * c:W2T * (c + 1)]), inc="eV"))
            pb.waits("Y", toks_xa + tcbs)
            for c in range(NCH):
                ts_ = pb.op("Y", lambda E, b=b, c=c, t0=t0: E.dma_start(
                    out=scrA[b, 128 * c:128 * (c + 1), t0:t0 + W2T],
                    in_=s2_xa[:, W2T * c:W2T * (c + 1)]), inc="dS2", n=16)
                tsb_ = pb.op("Y", lambda E, b=b, c=c, t0=t0: E.dma_start(
                    out=scrAb[b, 128 * c:128 * (c + 1), t0:t0 + W2T],
                    in_=s2_sb[:, W2T * c:W2T * (c + 1)]), inc="dS2", n=16)
                st_tile.append(ts_)
                st_tile.append(tsb_)
            st_s2.extend(st_tile)
            s2_xl2_prev = s2_xl2_gate
            s2_psum_prev = toks_xa
            s2_stb_prev = st_tile

    if tap == "scrA":
        return emit_tap(scrA)

    NST = T // 512

    def emit_attn_stage(l, src, srcb, dst, dstb):
        a_wqkv = at("a_wqkv", [128, NCH * 3072], BF16, 0)
        a_wo = at("a_wo", [128, NCH * 1024], BF16, 49152)
        a_xs = at("a_xs", [128, 2 * NCH * 512], BF16, 65536)
        a_qr = at("a_qr", [128, NCH * 512], BF16, 81920)
        a_kr = at("a_kr", [128, NCH * 512], BF16, 90112)
        a_q2 = at("a_q2", [128, 16 * 512], BF16, 98304)
        a_qf = at("a_qf", [128, NCH * 512], BF16, 114688)
        a_kroll = at("a_kroll", [128, NCH * 768], BF16, 122880)
        a_vroll = at("a_vroll", [128, 6 * 1280], BF16, 135168)
        a_e = at("a_e", [128, 2 * 3072], BF16, 150528)
        a_rnq = at("a_rnq", [16, 512], F32, 162816)
        a_rnk = at("a_rnk", [16, 512], F32, 164864)
        a_rnqb = at("a_rnqb", [32, 512], BF16, 166912)
        a_rnkb = at("a_rnkb", [32, 512], BF16, 167936)
        a_rnsb = at("a_rnsb", [128, 2 * 512], BF16, 168960)
        a_rs = at("a_rs", [128, 512], F32, 162816)
        a_rsb = at("a_rsb", [128, 512], BF16, 164864)
        a_w1b = at("a_w1b", [128, 2 * 512], BF16, 171008)
        a_w2b = at("a_w2b", [128, 2 * 512], BF16, 173056)
        a_ao = at("a_ao", [128, NCH * 512], BF16, 175104)
        a_xres = at("a_xres", [128, 2 * 512], F32, 183296)
        a_xnew = at("a_xnew", [128, 2 * 512], F32, 187392)
        a_tab = at("a_tab", [128, 2 * 4 * 512], BF16, 191488)
        a_mask = at("a_mask", [128, 6 * 512], BF16, 199680)
        a_xnb = at("a_xnb", [128, 2 * 512], BF16, 168960)
        a_hsc = at("a_hsc", [128, 2 * 128], BF16, 205824)
        prev = stage_barrier_toks()
        pb.waits("G", prev)
        pb.waits("V", prev)
        pb.waits("Y", prev)
        pb.waits("S", prev)
        th1 = pb.op("V", lambda E, l=l: E.tensor_scalar(
            a_hsc[:, 0:128], hind[:, :], iqs8sq_pp[:, l:l + 1], None, ALU.mult), inc="eV")
        th2 = pb.op("V", lambda E, l=l: E.tensor_scalar(
            a_hsc[:, 128:256], hind[:, :], iksc2_pp[:, l:l + 1], None, ALU.mult), inc="eV")
        hsc_toks = [th1, th2]
        wl = []
        for ci in range(NCH):
            wl.append(pb.op("G", lambda E, ci=ci: E.dma_start(
                out=a_wqkv[:, 3072 * ci:3072 * (ci + 1)], in_=wqkv[l, 128 * ci:128 * (ci + 1), :]), inc="dW", n=16))
            wl.append(pb.op("G", lambda E, ci=ci: E.dma_start(
                out=a_wo[:, 1024 * ci:1024 * (ci + 1)], in_=wo2[l, 128 * ci:128 * (ci + 1), :]), inc="dW", n=16))
        tmsk = pb.op("G", lambda E: E.memset(a_mask[:, :], 1.0), inc="eG")
        pb.wait("G", tmsk)
        for k in range(6):
            jbr = -256 + 128 * k
            col0 = 512 * k
            t2_ = pb.op("G", lambda E, col0=col0, jbr=jbr: E.affine_select(
                a_mask[:, col0:col0 + 512], a_mask[:, col0:col0 + 512],
                [[1, 512]], ALU.is_ge, 0.0, base=-jbr, channel_multiplier=-1), inc="eG")
            pb.wait("G", t2_)
            t3_ = pb.op("G", lambda E, col0=col0, jbr=jbr: E.affine_select(
                a_mask[:, col0:col0 + 512], a_mask[:, col0:col0 + 512],
                [[-1, 512]], ALU.is_ge, 0.0, base=jbr + 256, channel_multiplier=1), inc="eG")
            pb.wait("G", t3_)
        wl.append(pb.tok("eG"))
        tz = []
        tz.append(pb.op("V", lambda E: E.memset(a_rnqb[:, :], 0.0), inc="eV"))
        tz.append(pb.op("V", lambda E: E.memset(a_rnkb[:, :], 0.0), inc="eV"))
        for par in range(2):
            for k in (3, 4, 5):
                sr = 128 * (k - 2)
                tz.append(pb.op("V", lambda E, par=par, k=k, sr=sr: E.memset(
                    a_e[:, 3072 * par + 512 * k:3072 * par + 512 * k + sr], 0.0), inc="eV"))
        for sl in range(6):
            for p in range(NCH):
                tz.append(pb.op("V", lambda E, sl=sl, p=p: E.memset(
                    a_vroll[:, 1280 * sl + 160 * p + 64:1280 * sl + 160 * p + 65], 1.0), inc="eV"))
                tz.append(pb.op("V", lambda E, sl=sl, p=p: E.memset(
                    a_vroll[:, 1280 * sl + 160 * p + 65:1280 * sl + 160 * p + 96], 0.0), inc="eV"))
        wl += tz

        store_toks = []
        kv_readers = {}
        xs_cons = {0: [], 1: []}
        tab_cons = {0: [], 1: []}
        e_pv = {0: None, 1: None}
        stats_prev = []
        stb_prev = {}
        wo_last = None
        qf_readers = []
        qr_readers = []
        ao_norm_toks = []
        rcp_free = None
        prev_store = []
        sts = [(b, st) for b in range(B) for st in range(NST)]

        def emit_attn_loads(b, st, semn="dXc"):
            par = st % 2
            W0 = 512 * st
            xs0 = 512 * NCH * par
            pb.waits("S", xs_cons[par])
            lx = []
            for c in range(NCH):
                if c < 4 or W0 > 0:
                    t0r = W0 if c < 4 else W0 - 1
                    lx.append(pb.op("S", lambda E, b=b, c=c, t0r=t0r, xs0=xs0: E.dma_start(
                        out=a_xs[:, xs0 + 512 * c:xs0 + 512 * (c + 1)],
                        in_=srcb[b, 128 * c:128 * (c + 1), t0r:t0r + 512]), inc=semn, n=16))
                else:
                    lx.append(pb.op("S", lambda E, b=b, c=c, xs0=xs0: E.dma_start(
                        out=a_xs[:, xs0 + 512 * c + 1:xs0 + 512 * (c + 1)],
                        in_=srcb[b, 128 * c:128 * (c + 1), 0:511]), inc=semn, n=16))
                    pb.waits("V", xs_cons[par])
                    tm0 = pb.op("V", lambda E, c=c, xs0=xs0: E.memset(a_xs[:, xs0 + 512 * c:xs0 + 512 * c + 1], 0.0), inc="eV")
                    lx.append(tm0)
            return lx

        pend_lx = {0: emit_attn_loads(*sts[0], semn="dXe")}
        for sti, (b, st) in enumerate(sts):
            if True:
                W0 = 512 * st
                par = st % 2
                k0 = 2 if st == 0 else 0
                pcs = pieces_for(k0)
                xs0 = 512 * NCH * par
                if sti + 1 < len(sts):
                    pend_lx[sti + 1] = emit_attn_loads(*sts[sti + 1], semn=("dXe" if (sti + 1) % 2 == 0 else "dXo"))
                lx = pend_lx.pop(sti)
                pb.waits("Y", tab_cons[par])
                ltabs = []
                for ti in range(4):
                    ltabs.append(pb.op("Y", lambda E, par=par, W0=W0, ti=ti: E.dma_start(
                        out=a_tab[:, 2048 * par + 512 * ti:2048 * par + 512 * (ti + 1)],
                        in_=tabs[ti, :, W0:W0 + 512]), inc="dTb", n=16))
                ltab = ltabs[-1]

                def tabap(ti, par=par):
                    return a_tab[:, 2048 * par + 512 * ti:2048 * par + 512 * (ti + 1)]

                pb.waits("T", lx + wl)
                pb.waits("G", stats_prev)
                qev = []
                sq_toks = []
                xs_cons[par] = []
                for c in range(NCH):
                    for (iw, coff, bk, evdst, scal) in (
                            (0, 0, (2 * c) % 4, a_qr, qs8_pp[:, l:l + 1]),
                            (1, 1024, (2 * c + 1) % 4, a_kr, ksc_pp[:, l:l + 1])):
                        pb.waits("T", qr_readers)
                        for ci in range(NCH):
                            tk = pb.op("T", lambda E, c=c, ci=ci, coff=coff, bk=bk, xs0=xs0: E.matmul(
                                bank(bk), a_wqkv[:, 3072 * ci + coff + 128 * c:3072 * ci + coff + 128 * (c + 1)],
                                a_xs[:, xs0 + 512 * ci:xs0 + 512 * (ci + 1)],
                                start=(ci == 0), stop=(ci == NCH - 1)),
                                inc=("eT" if ci == NCH - 1 else None))
                        xs_cons[par].append(tk)
                        pb.waits("V", [tk])
                        tev = pb.op("V", lambda E, c=c, bk=bk, evdst=evdst, scal=scal: E.tensor_scalar(
                            evdst[:, 512 * c:512 * (c + 1)], bank(bk), scal, None, ALU.mult), inc="eV")
                        qev.append(tev)
                        pb.waits("G", [tev])
                        tsq = pb.op("G", lambda E, c=c, iw=iw, evdst=evdst: E.tensor_mul(
                            a_q2[:, 512 * (8 * iw + c):512 * (8 * iw + c + 1)],
                            evdst[:, 512 * c:512 * (c + 1)],
                            evdst[:, 512 * c:512 * (c + 1)]), inc="eG")
                        sq_toks.append(tsq)
                        if len(qev) >= 4:
                            pb.waits("T", [qev[-4]])
                qr_readers = []
                vev = []
                for sub in range(4):
                    slot = ((W0 + 128 * sub) // 128) % 6
                    pb.waits("V", kv_readers.get((b, slot), [])[-2:])
                    for (ps, voff) in ((psA, 2048), (psB, 2560)):
                        for ci in range(NCH):
                            tk = pb.op("T", lambda E, sub=sub, ci=ci, voff=voff, ps=ps, xs0=xs0: E.matmul(
                                ps[:, :], a_xs[:, xs0 + 512 * ci + 128 * sub:xs0 + 512 * ci + 128 * (sub + 1)],
                                a_wqkv[:, 3072 * ci + voff:3072 * ci + voff + 512],
                                start=(ci == 0), stop=(ci == NCH - 1)),
                                inc=("eT" if ci == NCH - 1 else None))
                        xs_cons[par].append(tk)
                        pb.waits("V", [tk] + tz)
                        for pp in range(4):
                            p = pp + (0 if voff == 2048 else 4)
                            tv = pb.op("V", lambda E, slot=slot, p=p, pp=pp, ps=ps: E.tensor_copy(
                                a_vroll[:, 1280 * slot + 160 * p:1280 * slot + 160 * p + 64],
                                ps[:, 128 * pp:128 * pp + 64]), inc="eV")
                            tv2 = pb.op("V", lambda E, slot=slot, p=p, pp=pp, ps=ps: E.tensor_copy(
                                a_vroll[:, 1280 * slot + 160 * p + 96:1280 * slot + 160 * p + 160],
                                ps[:, 128 * pp + 64:128 * (pp + 1)]), inc="eV")
                            vev += [tv, tv2]
                        pb.waits("T", [vev[-1]])
                t_sq = None
                rn_casts = {}
                stats_cur = []
                for (iw, stat_bk, rnf, rnb) in ((0, 4, a_rnq, a_rnqb), (1, 5, a_rnk, a_rnkb)):
                    pb.waits("T", sq_toks + hsc_toks)
                    for c in range(NCH):
                        t_sq = pb.op("T", lambda E, c=c, iw=iw, stat_bk=stat_bk: E.matmul(
                            bank(stat_bk)[0:16, :], a_hsc[:, 128 * iw + 16 * c:128 * iw + 16 * (c + 1)],
                            a_q2[:, 512 * (8 * iw + c):512 * (8 * iw + c + 1)],
                            start=(c == 0), stop=(c == NCH - 1)), inc="eT")
                    stats_cur.append(t_sq)
                    pb.waits("V", [t_sq])
                    tr1 = pb.op("V", lambda E, rnf=rnf, stat_bk=stat_bk: E.tensor_scalar_max(
                        rnf[:, :], bank(stat_bk)[0:16, :], 1e-24), inc="eV")
                    pb.waits("S", [tr1])
                    tr3 = pb.op("S", lambda E, rnf=rnf: E.activation(rnf[:, :], rnf[:, :], AF.Ln), inc="eS")
                    pb.wait("S", tr3)
                    tr5 = pb.op("S", lambda E, rnf=rnf: E.activation(rnf[:, :], rnf[:, :], AF.Exp, scale=-0.5), inc="eS")
                    pb.waits("V", [tr5])
                    pb.waits("V", tz)
                    tr7 = pb.op("V", lambda E, rnf=rnf, rnb=rnb: E.tensor_copy(rnb[0:16, :], rnf[:, :]), inc="eV")
                    rn_casts[iw] = (tr1, tr7)
                rot_done = []
                pb.waits("T", vev)
                pb.waits("T", qf_readers)
                pb.waits("G", qev)
                items = []
                for c in range(NCH):
                    items.append((c, a_qr, a_rnqb, 0, 1, "qf", rn_casts[0]))
                    items.append((c, a_kr, a_rnkb, 2, 3, "kr", rn_casts[1]))
                slots4 = [((W0 // 128) + s) % 6 for s in range(4)]
                kr_write_gate = [kv_readers.get((b, s), [])[-1:] for s in slots4]
                prev_exp_ev = {}
                prev_rotmm = {}
                rnsb_free = {}
                for it, (c, evsrc, rnb, c_t, s_t, dstsel, casts) in enumerate(items):
                    ip = it % 2
                    xbk = 4 + ip
                    pb.waits("T", [casts[1], prev_exp_ev.get(ip)])
                    tx = pb.op("T", lambda E, c=c, rnb=rnb, xbk=xbk: E.matmul(
                        bank(xbk), ind16[:, 128 * c:128 * (c + 1)], rnb[0:16, :],
                        start=True, stop=True), inc="eT")
                    pb.waits("S", [tx, casts[1], rnsb_free.get(ip), stb_prev.get(ip)])
                    txe = pb.op("S", lambda E, ip=ip, xbk=xbk: E.activation(
                        a_rnsb[:, 512 * ip:512 * (ip + 1)], bank(xbk), AF.Copy), inc="eS")
                    prev_exp_ev[ip] = txe
                    tab_c = tabap(c_t)
                    tab_s = tabap(s_t)
                    pb.waits("G", [ltab, prev_rotmm.get(ip)] + qev)
                    tw1_ = pb.op("G", lambda E, c=c, evsrc=evsrc, tab_c=tab_c, ip=ip: E.tensor_tensor(
                        a_w1b[:, 512 * ip:512 * (ip + 1)], evsrc[:, 512 * c:512 * (c + 1)], tab_c, ALU.mult), inc="eG")
                    tw2_ = pb.op("G", lambda E, c=c, evsrc=evsrc, tab_s=tab_s, ip=ip: E.tensor_tensor(
                        a_w2b[:, 512 * ip:512 * (ip + 1)], evsrc[:, 512 * c:512 * (c + 1)], tab_s, ALU.mult), inc="eG")
                    tab_cons[par] = [tw2_]
                    qr_readers.append(tw2_)
                    rps = psA if ip == 0 else psB
                    pb.waits("T", [tw1_, tw2_])
                    pb.op("T", lambda E, ip=ip, rps=rps: E.matmul(rps[:, :], ident_bf[:, :], a_w1b[:, 512 * ip:512 * (ip + 1)], start=True, stop=False))
                    trr = pb.op("T", lambda E, ip=ip, rps=rps: E.matmul(rps[:, :], r2_bf[:, :], a_w2b[:, 512 * ip:512 * (ip + 1)], start=False, stop=True), inc="eT")
                    prev_rotmm[ip] = trr
                    pb.waits("V", [trr, txe])
                    if dstsel == "qf":
                        tev = pb.op("V", lambda E, c=c, rps=rps, ip=ip: E.tensor_tensor(
                            a_qf[:, 512 * c:512 * (c + 1)], rps[:, :],
                            a_rnsb[:, 512 * ip:512 * (ip + 1)], ALU.mult), inc="eV")
                    else:
                        pb.waits("V", kr_write_gate[0] + kr_write_gate[1] + kr_write_gate[2] + kr_write_gate[3])
                        runs = []
                        run = [slots4[0], 0, 1]
                        for s in range(1, 4):
                            if slots4[s] == run[0] + run[2]:
                                run[2] += 1
                            else:
                                runs.append(tuple(run))
                                run = [slots4[s], s, 1]
                        runs.append(tuple(run))
                        for (sl0, s0, ln_) in runs:
                            tev = pb.op("V", lambda E, c=c, sl0=sl0, s0=s0, ln_=ln_, rps=rps, ip=ip: E.tensor_tensor(
                                a_kroll[:, 768 * c + 128 * sl0:768 * c + 128 * (sl0 + ln_)],
                                rps[:, 128 * s0:128 * (s0 + ln_)],
                                a_rnsb[:, 512 * ip + 128 * s0:512 * ip + 128 * (s0 + ln_)], ALU.mult), inc="eV")
                    rot_done.append(tev)
                    prev_rotmm[ip] = tev
                    rnsb_free[ip] = tev
                heads = [(c, h) for c in range(NCH) for h in (0, 1)]
                qf_readers = []
                prev_hd = None
                pend_tail = None
                ao_norm_toks = []
                prev_exp_tok = None
                prev_mask = {0: None, 1: None}
                for idx, (c, h01) in enumerate(heads):
                    epar = idx % 2
                    pvps = psA if h01 == 0 else psB
                    pb.waits("T", rot_done + [prev_exp_tok])
                    sim_toks = []
                    for (k, jbr, s_rel, w) in pcs:
                        slot = ((W0 + jbr) // 128) % 6
                        sr = max(0, 128 * (k - 2))
                        tk = pb.op("T", lambda E, c=c, h01=h01, k=k, slot=slot, sr=sr: E.matmul(
                            ps6[:, 512 * k + sr:512 * (k + 1)],
                            a_kroll[64 * h01:64 * h01 + 64, 768 * c + 128 * slot:768 * c + 128 * (slot + 1)],
                            a_qf[64 * h01:64 * h01 + 64, 512 * c + sr:512 * (c + 1)],
                            start=True, stop=True), inc="eT")
                        sim_toks.append(tk)
                        kv_readers.setdefault((b, slot), []).append(tk)
                        qf_readers.append(tk)
                    if pend_tail is not None:
                        pend_tail()
                        pend_tail = None
                    if prev_hd is not None:
                        pb.waits("T", [prev_hd["mask"]] + ao_norm_toks[-2:])
                        npc = len(pcs)
                        pvtok = None
                        for i2, (k, jbr, s_rel, w) in enumerate(pcs):
                            slot = ((W0 + jbr) // 128) % 6
                            aoff = 0 if prev_hd["h01"] == 0 else 32
                            pvtok = pb.op("T", lambda E, c2=prev_hd["c"], k=k, slot=slot, aoff=aoff,
                                          pv=prev_hd["pvps"], ep=prev_hd["epar"], i2=i2, npc=npc: E.matmul(
                                pv[:, :],
                                a_vroll[:, 1280 * slot + 160 * c2 + aoff:1280 * slot + 160 * c2 + aoff + 128],
                                a_e[:, 3072 * ep + 512 * k:3072 * ep + 512 * (k + 1)],
                                start=(i2 == 0), stop=(i2 == npc - 1)), inc="eT")
                            kv_readers.setdefault((b, slot), []).append(pvtok)
                        e_pv[prev_hd["epar"]] = pvtok
                        if prev_hd["h01"] == 1:
                            cc = prev_hd["c"]
                            pb.waits("S", [pvtok])
                            ts1_ = pb.op("S", lambda E: E.activation(a_rs[64:65, :], psA[64:65, :], AF.Ln), inc="eS")
                            ts2_ = pb.op("S", lambda E: E.activation(a_rs[32:33, :], psB[32:33, :], AF.Ln), inc="eS")
                            pb.waits("S", [ts1_, ts2_])
                            ts3_ = pb.op("S", lambda E: E.activation(a_rsb[64:65, :], a_rs[64:65, :], AF.Exp, scale=-1.0), inc="eS")
                            ts4_ = pb.op("S", lambda E: E.activation(a_rsb[32:33, :], a_rs[32:33, :], AF.Exp, scale=-1.0), inc="eS")
                            pb.waits("V", [pvtok, wo_last])
                            te1 = pb.op("V", lambda E, cc=cc: E.tensor_copy(a_ao[0:64, 512 * cc:512 * (cc + 1)], psA[0:64, :]), inc="eV")
                            te2 = pb.op("V", lambda E, cc=cc: E.tensor_copy(a_ao[64:128, 512 * cc:512 * (cc + 1)], psB[64:128, :]), inc="eV")

                            def mk_tail(cc=cc, ts3_=ts3_, ts4_=ts4_, te1=te1, te2=te2):
                                def tail():
                                    pb.waits("T", [ts3_, ts4_, te1, te2])
                                    tb1 = pb.op("T", lambda E: E.matmul(psA[:, :], onesrow64[64:65, :], a_rsb[64:65, :], start=True, stop=True), inc="eT")
                                    tb2 = pb.op("T", lambda E: E.matmul(psB[:, :], onesrow64[32:33, :], a_rsb[32:33, :], start=True, stop=True), inc="eT")
                                    pb.waits("V", [tb1, tb2])
                                    tn1 = pb.op("V", lambda E, cc=cc: E.tensor_tensor(
                                        a_ao[0:64, 512 * cc:512 * (cc + 1)], a_ao[0:64, 512 * cc:512 * (cc + 1)],
                                        psA[0:64, :], ALU.mult), inc="eV")
                                    tn2 = pb.op("V", lambda E, cc=cc: E.tensor_tensor(
                                        a_ao[64:128, 512 * cc:512 * (cc + 1)], a_ao[64:128, 512 * cc:512 * (cc + 1)],
                                        psB[64:128, :], ALU.mult), inc="eV")
                                    ao_norm_toks.extend([tn1, tn2])
                                    return [tn1, tn2]
                                return tail
                            pend_tail = mk_tail()
                    pb.waits("S", sim_toks + [e_pv[epar]])
                    exp_toks = []
                    exp_toks.append(pb.op("S", lambda E, k0=k0, epar=epar, l=l: E.activation(
                        a_e[:, 3072 * epar + 512 * k0:3072 * epar + 1536],
                        ps6[:, 512 * k0:1536], AF.Exp, bias=negC_pp[:, l:l + 1]), inc="eS"))
                    for k in (3, 4, 5):
                        sr = 128 * (k - 2)
                        exp_toks.append(pb.op("S", lambda E, k=k, sr=sr, epar=epar, l=l: E.activation(
                            a_e[:, 3072 * epar + 512 * k + sr:3072 * epar + 512 * (k + 1)],
                            ps6[:, 512 * k + sr:512 * (k + 1)], AF.Exp, bias=negC_pp[:, l:l + 1]), inc="eS"))
                    prev_exp_tok = exp_toks[-1]
                    pb.waits("V", exp_toks + [prev_mask[epar]])
                    t_mk = pb.op("V", lambda E, k0=k0, epar=epar: E.tensor_tensor(
                        a_e[:, 3072 * epar + 512 * k0:3072 * epar + 3072],
                        a_e[:, 3072 * epar + 512 * k0:3072 * epar + 3072],
                        a_mask[:, 512 * k0:3072], ALU.mult), inc="eV")
                    prev_mask[epar] = t_mk
                    prev_hd = {"c": c, "h01": h01, "epar": epar, "pvps": pvps, "mask": t_mk}
                pb.waits("T", [prev_hd["mask"]] + ao_norm_toks[-2:])
                npc = len(pcs)
                pvtok = None
                for i2, (k, jbr, s_rel, w) in enumerate(pcs):
                    slot = ((W0 + jbr) // 128) % 6
                    pvtok = pb.op("T", lambda E, c2=prev_hd["c"], k=k, slot=slot,
                                  ep=prev_hd["epar"], i2=i2, npc=npc: E.matmul(
                        psB[:, :],
                        a_vroll[:, 1280 * slot + 160 * c2 + 32:1280 * slot + 160 * c2 + 160],
                        a_e[:, 3072 * ep + 512 * k:3072 * ep + 512 * (k + 1)],
                        start=(i2 == 0), stop=(i2 == npc - 1)), inc="eT")
                    kv_readers.setdefault((b, slot), []).append(pvtok)
                e_pv[prev_hd["epar"]] = pvtok
                if pend_tail is not None:
                    pend_tail()
                    pend_tail = None
                cc = prev_hd["c"]
                pb.waits("S", [pvtok])
                ts1_ = pb.op("S", lambda E: E.activation(a_rs[64:65, :], psA[64:65, :], AF.Ln), inc="eS")
                ts2_ = pb.op("S", lambda E: E.activation(a_rs[32:33, :], psB[32:33, :], AF.Ln), inc="eS")
                pb.waits("S", [ts1_, ts2_])
                ts3_ = pb.op("S", lambda E: E.activation(a_rsb[64:65, :], a_rs[64:65, :], AF.Exp, scale=-1.0), inc="eS")
                ts4_ = pb.op("S", lambda E: E.activation(a_rsb[32:33, :], a_rs[32:33, :], AF.Exp, scale=-1.0), inc="eS")
                pb.waits("V", [pvtok, wo_last])
                te1 = pb.op("V", lambda E, cc=cc: E.tensor_copy(a_ao[0:64, 512 * cc:512 * (cc + 1)], psA[0:64, :]), inc="eV")
                te2 = pb.op("V", lambda E, cc=cc: E.tensor_copy(a_ao[64:128, 512 * cc:512 * (cc + 1)], psB[64:128, :]), inc="eV")
                pb.waits("T", [ts3_, ts4_, te1, te2])
                tb1 = pb.op("T", lambda E: E.matmul(psA[:, :], onesrow64[64:65, :], a_rsb[64:65, :], start=True, stop=True), inc="eT")
                tb2 = pb.op("T", lambda E: E.matmul(psB[:, :], onesrow64[32:33, :], a_rsb[32:33, :], start=True, stop=True), inc="eT")
                pb.waits("V", [tb1, tb2])
                tn1 = pb.op("V", lambda E, cc=cc: E.tensor_tensor(
                    a_ao[0:64, 512 * cc:512 * (cc + 1)], a_ao[0:64, 512 * cc:512 * (cc + 1)],
                    psA[0:64, :], ALU.mult), inc="eV")
                tn2 = pb.op("V", lambda E, cc=cc: E.tensor_tensor(
                    a_ao[64:128, 512 * cc:512 * (cc + 1)], a_ao[64:128, 512 * cc:512 * (cc + 1)],
                    psB[64:128, :], ALU.mult), inc="eV")
                ao_norm_toks.extend([tn1, tn2])
                st_tile = []
                lxr = {}
                st_prev = {}
                pb.waits("Y", prev_store)
                lxr[0] = pb.op("Y", lambda E, b=b, W0=W0: E.dma_start(
                    out=a_xres[:, 0:512], in_=src[b, 0:128, W0:W0 + 512]), inc="dXr", n=16)
                pb.waits("T", ao_norm_toks + [tn1, tn2])
                for c in range(NCH):
                    if c + 1 < NCH:
                        lxr[(c + 1) % 2] = pb.op("Y", lambda E, b=b, c=c, W0=W0: E.dma_start(
                            out=a_xres[:, 512 * ((c + 1) % 2):512 * ((c + 1) % 2) + 512],
                            in_=src[b, 128 * (c + 1):128 * (c + 2), W0:W0 + 512]), inc="dXr", n=16)
                    bk = 2 + c % 2
                    for ci in range(NCH):
                        tk = pb.op("T", lambda E, c=c, ci=ci, bk=bk: E.matmul(
                            bank(bk), a_wo[:, 1024 * ci + 128 * c:1024 * ci + 128 * (c + 1)],
                            a_ao[:, 512 * ci:512 * (ci + 1)],
                            start=(ci == 0), stop=(ci == NCH - 1)),
                            inc=("eT" if ci == NCH - 1 else None))
                    wo_last = tk
                    pb.waits("V", [tk, lxr[c % 2], st_prev.get(c % 2)])
                    te = pb.op("V", lambda E, c=c, bk=bk: E.scalar_tensor_tensor(
                        a_xnew[:, 512 * (c % 2):512 * (c % 2) + 512], bank(bk), 1.0,
                        a_xres[:, 512 * (c % 2):512 * (c % 2) + 512], ALU.mult, ALU.add), inc="eV")
                    pb.waits("V", [stb_prev.get(c % 2), te, prev_exp_ev.get(c % 2)])
                    tcb = pb.op("V", lambda E, c=c: E.tensor_copy(
                        a_xnb[:, 512 * (c % 2):512 * (c % 2) + 512],
                        a_xnew[:, 512 * (c % 2):512 * (c % 2) + 512]), inc="eV")
                    pb.waits("T", [te])
                    pb.waits("Y", [te, tcb])
                    ts_ = pb.op("Y", lambda E, b=b, c=c, W0=W0: E.dma_start(
                        out=dst[b, 128 * c:128 * (c + 1), W0:W0 + 512],
                        in_=a_xnew[:, 512 * (c % 2):512 * (c % 2) + 512]), inc="dSt", n=16)
                    tsb_ = pb.op("Y", lambda E, b=b, c=c, W0=W0: E.dma_start(
                        out=dstb[b, 128 * c:128 * (c + 1), W0:W0 + 512],
                        in_=a_xnb[:, 512 * (c % 2):512 * (c % 2) + 512]), inc="dSt", n=16)
                    st_tile.append(ts_)
                    st_tile.append(tsb_)
                    store_toks.append(ts_)
                    st_prev[c % 2] = ts_
                    stb_prev[c % 2] = tsb_
                prev_store = st_tile
                stats_prev = stats_cur
        return store_toks


    def emit_ffn_stage(l, src, srcb, dst, dstb):
        f_w1 = at("f_w1", [128, NCH * D2], BF16, 0)
        f_w2 = at("f_w2", [128, 22 * 1024], BF16, 87360)
        f_xs = at("f_xs", [128, 2 * NCH * 512], BF16, 132416)
        f_xn = at("f_xn", [128, 2 * NCH * 512], BF16, 148800)
        f_h = at("f_h", [128, 22 * 512], BF16, 165184)
        f_sq = at("f_sq", [128, 512], BF16, 187712)
        f_ms = at("f_ms", [1, 3 * 512], BF16, 188736)
        f_rstd = at("f_rstd", [1, 512], BF16, 191808)
        f_cbr = at("f_cbr", [2, 512], BF16, 192832)
        f_gb = at("f_gb", [2, C], BF16, 193856)
        f_gpp = at("f_gpp", [128, NCH], F32, 195904)
        f_bpp = at("f_bpp", [128, NCH], F32, 195968)
        f_b1a = at("f_b1a", [128, 22], F32, 196032)
        f_b1g = at("f_b1g", [128, 22], F32, 196160)
        f_b2 = at("f_b2", [128, NCH], F32, 196288)
        f_gt = at("f_gt", [128, 2 * 512], BF16, 196352)
        f_xres = at("f_xres", [128, 2 * 512], F32, 198400)
        f_xnew = at("f_xnew", [128, 2 * 512], F32, 202496)
        f_xnb = at("f_xnb", [128, 2 * 512], BF16, 196352)
        prev = stage_barrier_toks()
        pb.waits("G", prev)
        pb.waits("Y", prev)
        pb.waits("S", prev)
        wl = []
        for ci in range(NCH):
            wl.append(pb.op("G", lambda E, ci=ci: E.dma_start(
                out=f_w1[:, D2 * ci:D2 * (ci + 1)], in_=w1[l, 128 * ci:128 * (ci + 1), :]), inc="dW", n=16))
        for j in range(22):
            wj = min(128, DI - 128 * j)
            wl.append(pb.op("G", lambda E, j=j, wj=wj: E.dma_start(
                out=f_w2[0:wj, 1024 * j:1024 * (j + 1)], in_=w2[l, 128 * j:128 * j + wj, :]), inc="dW", n=16))
            wl.append(pb.op("G", lambda E, j=j, wj=wj, l=l: E.dma_start(
                out=f_b1a[0:wj, j:j + 1], in_=b1_in[l, 128 * j:128 * j + wj]), inc="dW", n=16))
            wl.append(pb.op("G", lambda E, j=j, wj=wj, l=l: E.dma_start(
                out=f_b1g[0:wj, j:j + 1], in_=b1_in[l, DI + 128 * j:DI + 128 * j + wj]), inc="dW", n=16))
        for c in range(NCH):
            wl.append(pb.op("G", lambda E, c=c, l=l: E.dma_start(
                out=f_gpp[:, c:c + 1], in_=lng_in[l, 128 * c:128 * (c + 1)]), inc="dW", n=16))
            wl.append(pb.op("G", lambda E, c=c, l=l: E.dma_start(
                out=f_bpp[:, c:c + 1], in_=lnb_in[l, 128 * c:128 * (c + 1)]), inc="dW", n=16))
            wl.append(pb.op("G", lambda E, c=c, l=l: E.dma_start(
                out=f_b2[:, c:c + 1], in_=b2_in[l, 128 * c:128 * (c + 1)]), inc="dW", n=16))
        wl.append(pb.op("G", lambda E, l=l: E.dma_start(out=f_gb[0:1, :], in_=lng_in[l, None, :]), inc="dW", n=16))
        wl.append(pb.op("G", lambda E, l=l: E.dma_start(out=f_gb[1:2, :], in_=lnb_in[l, None, :]), inc="dW", n=16))
        pb.waits("Y", [tones512])
        wl.append(pb.op("Y", lambda E: E.dma_start(out=f_cbr[1:2, :], in_=ones512_b[0:1, :]), inc="dO1", n=16))

        store_toks = []
        xs_cons = {0: [], 1: []}
        xn_cons = {0: [], 1: []}
        prev_store = []
        prev_ctx = None
        h_free = None
        gt_free = {0: None, 1: None}
        tg_last = {}
        f_stb_prev = {}
        f_psum_prev = []

        def emit_w1w2(ctx):
            nonlocal h_free, prev_store
            b2_, st2, par2, xn_toks = ctx
            W0 = 512 * st2
            xn0 = 512 * NCH * par2
            h_toks = []
            pb.waits("T", xn_toks + wl)
            for j in range(22):
                wj = min(128, DI - 128 * j)
                abk = j % 3
                gbk = 3 + j % 3
                for ci in range(NCH):
                    tka = pb.op("T", lambda E, j=j, ci=ci, abk=abk, wj=wj, xn0=xn0: E.matmul(
                        bank(abk)[0:wj, :],
                        f_w1[:, D2 * ci + 128 * j:D2 * ci + 128 * j + wj],
                        f_xn[:, xn0 + 512 * ci:xn0 + 512 * (ci + 1)],
                        start=(ci == 0), stop=(ci == NCH - 1)),
                        inc=("eT" if ci == NCH - 1 else None))
                for ci in range(NCH):
                    tkg = pb.op("T", lambda E, j=j, ci=ci, gbk=gbk, wj=wj, xn0=xn0: E.matmul(
                        bank(gbk)[0:wj, :],
                        f_w1[:, D2 * ci + DI + 128 * j:D2 * ci + DI + 128 * j + wj],
                        f_xn[:, xn0 + 512 * ci:xn0 + 512 * (ci + 1)],
                        start=(ci == 0), stop=(ci == NCH - 1)),
                        inc=("eT" if ci == NCH - 1 else None))
                xn_cons[par2] += [tka, tkg]
                gp = j % 2
                pb.waits("S", [tkg, gt_free[gp], f_stb_prev.get(gp)])
                tg = pb.op("S", lambda E, j=j, gbk=gbk, wj=wj, gp=gp: E.activation(
                    f_gt[0:wj, 512 * gp:512 * gp + 512], bank(gbk)[0:wj, :], AF.Gelu, bias=f_b1g[0:wj, j:j + 1]), inc="eS")
                tg_last[gp] = tg
                pb.waits("V", [tka, tg, h_free])
                th = pb.op("V", lambda E, j=j, abk=abk, wj=wj, gp=gp: E.scalar_tensor_tensor(
                    f_h[0:wj, 512 * j:512 * (j + 1)], bank(abk)[0:wj, :],
                    f_b1a[0:wj, j:j + 1], f_gt[0:wj, 512 * gp:512 * gp + 512], ALU.add, ALU.mult), inc="eV")
                h_toks.append(th)
                gt_free[gp] = th
                if len(h_toks) >= 3:
                    pb.waits("T", [h_toks[-3]])
            pb.waits("T", h_toks)
            lxr = {}
            st_prev = {}
            pb.waits("Y", prev_store)
            lxr[0] = pb.op("Y", lambda E, b2_=b2_, W0=W0: E.dma_start(
                out=f_xres[:, 0:512], in_=src[b2_, 0:128, W0:W0 + 512]), inc="dXr", n=16)
            st_tile = []
            wlast = None
            for c in range(NCH):
                if c + 1 < NCH:
                    lxr[(c + 1) % 2] = pb.op("Y", lambda E, b2_=b2_, c=c, W0=W0: E.dma_start(
                        out=f_xres[:, 512 * ((c + 1) % 2):512 * ((c + 1) % 2) + 512],
                        in_=src[b2_, 128 * (c + 1):128 * (c + 2), W0:W0 + 512]), inc="dXr", n=16)
                bk = c % 2
                for j in range(22):
                    wj = min(128, DI - 128 * j)
                    tk = pb.op("T", lambda E, c=c, j=j, bk=bk, wj=wj: E.matmul(
                        bank(bk), f_w2[0:wj, 1024 * j + 128 * c:1024 * j + 128 * (c + 1)],
                        f_h[0:wj, 512 * j:512 * (j + 1)],
                        start=(j == 0), stop=(j == 21)),
                        inc=("eT" if j == 21 else None))
                wlast = tk
                pb.waits("V", [tk, lxr[c % 2], st_prev.get(c % 2)])
                te = pb.op("V", lambda E, c=c, bk=bk: E.scalar_tensor_tensor(
                    f_xnew[:, 512 * (c % 2):512 * (c % 2) + 512], bank(bk),
                    f_b2[:, c:c + 1], f_xres[:, 512 * (c % 2):512 * (c % 2) + 512],
                    ALU.add, ALU.add), inc="eV")
                pb.waits("V", [f_stb_prev.get(c % 2), te, tg_last.get(c % 2)])
                tcb = pb.op("V", lambda E, c=c: E.tensor_copy(
                    f_xnb[:, 512 * (c % 2):512 * (c % 2) + 512],
                    f_xnew[:, 512 * (c % 2):512 * (c % 2) + 512]), inc="eV")
                pb.waits("T", [te])
                pb.waits("Y", [te, tcb])
                ts_ = pb.op("Y", lambda E, b2_=b2_, c=c, W0=W0: E.dma_start(
                    out=dst[b2_, 128 * c:128 * (c + 1), W0:W0 + 512],
                    in_=f_xnew[:, 512 * (c % 2):512 * (c % 2) + 512]), inc="dSt", n=16)
                tsb_ = pb.op("Y", lambda E, b2_=b2_, c=c, W0=W0: E.dma_start(
                    out=dstb[b2_, 128 * c:128 * (c + 1), W0:W0 + 512],
                    in_=f_xnb[:, 512 * (c % 2):512 * (c % 2) + 512]), inc="dSt", n=16)
                st_tile.append(ts_)
                st_tile.append(tsb_)
                store_toks.append(ts_)
                st_prev[c % 2] = ts_
                f_stb_prev[c % 2] = tsb_
            h_free = wlast
            prev_store = st_tile

        sts = [(b, st) for b in range(B) for st in range(NST)]

        def emit_loads(bi, sti, semn="dXc"):
            par = sti % 2
            xs0 = 512 * NCH * par
            W0 = 512 * sti
            pb.waits("S", xs_cons[par])
            lx = []
            for c in range(NCH):
                if c < 4 or W0 > 0:
                    t0r = W0 if c < 4 else W0 - 1
                    lx.append(pb.op("S", lambda E, bi=bi, c=c, t0r=t0r, xs0=xs0: E.dma_start(
                        out=f_xs[:, xs0 + 512 * c:xs0 + 512 * (c + 1)],
                        in_=srcb[bi, 128 * c:128 * (c + 1), t0r:t0r + 512]), inc=semn, n=16))
                else:
                    lx.append(pb.op("S", lambda E, bi=bi, c=c, xs0=xs0: E.dma_start(
                        out=f_xs[:, xs0 + 512 * c + 1:xs0 + 512 * (c + 1)],
                        in_=srcb[bi, 128 * c:128 * (c + 1), 0:511]), inc=semn, n=16))
                    pb.waits("V", xs_cons[par])
                    tm0 = pb.op("V", lambda E, c=c, xs0=xs0: E.memset(f_xs[:, xs0 + 512 * c:xs0 + 512 * c + 1], 0.0), inc="eV")
                    lx.append(tm0)
            return lx

        pend_loads = {}
        pend_loads[0] = emit_loads(*sts[0], semn="dXe")
        for i, (b, st) in enumerate(sts):
            par = st % 2
            if i + 1 < len(sts):
                pend_loads[i + 1] = emit_loads(*sts[i + 1], semn=("dXe" if (i + 1) % 2 == 0 else "dXo"))
            lx = pend_loads.pop(i)
            pb.waits("V", lx + wl + xn_cons[par])
            pb.waits("S", lx)
            pb.waits("T", lx + wl)
            pb.waits("G", lx + xn_cons[par])
            xs0 = 512 * NCH * par
            xs_view = f_xs[:, xs0:xs0 + 512 * NCH]
            xn_view = f_xn[:, 512 * NCH * par:512 * NCH * par + 512 * NCH]
            toks_xn, _ = ln_stats_apply(
                xs_view, xs_view, xn_view, f_gb, f_gpp, lx + f_psum_prev, f_sq, f_ms,
                (f_rstd, f_cbr), width=512, bpp_ap=f_bpp)
            f_psum_prev = toks_xn
            xs_cons[par] = [pb.tok("eT"), pb.tok("eG"), pb.tok("eV")]
            xn_cons[par] = []
            if prev_ctx is not None:
                emit_w1w2(prev_ctx)
            prev_ctx = (b, st, par, toks_xn)
        emit_w1w2(prev_ctx)
        return store_toks

    cur = scrA if not skip_pre else x_in
    curb = scrAb
    for l in range(DEPTH):
        emit_attn_stage(l, cur, curb, scrP, scrPb)
        if tap == f"lay{l}a":
            return emit_tap(scrP)
        nxt, nxtb = (scrQ, scrQb) if l % 2 == 0 else (scr1, scr1b)
        emit_ffn_stage(l, scrP, scrPb, nxt, nxtb)
        if tap == f"lay{l}f":
            return emit_tap(nxt)
        cur, curb = nxt, nxtb

    fl_z = at("fl_z", [128, NCH * 512], F32, 0)
    fl_xa = at("fl_xa", [128, NCH * 512], F32, 16 * 1024)
    fl_s = at("fl_s", [128, NCH * 512], F32, 32 * 1024)
    fl_sb = at("fl_sb", [128, NCH * 512], BF16, 48 * 1024)
    fl_sq = at("fl_sq", [128, 512], BF16, 56 * 1024)
    fl_ms = at("fl_ms", [1, 4 * 512], F32, 57 * 1024)
    fl_o = at("fl_o", [128, NCH * 512], F32, 66 * 1024)
    prevf = stage_barrier_toks()
    pb.waits("Y", prevf)
    cons_prev = []
    final_st = []
    for b in range(B):
        for st in range(NST):
            W0 = 512 * st
            pb.waits("Y", cons_prev)
            lx = []
            for c in range(NCH):
                lx.append(pb.op("Y", lambda E, b=b, c=c, W0=W0: E.dma_start(
                    out=fl_z[:, 512 * c:512 * (c + 1)], in_=cur[b, 128 * c:128 * (c + 1), W0:W0 + 512]), inc="dX", n=16))
                lx.append(pb.op("Y", lambda E, b=b, c=c, W0=W0: E.dma_start(
                    out=fl_xa[:, 512 * c:512 * (c + 1)], in_=scrA[b, 128 * c:128 * (c + 1), W0:W0 + 512]), inc="dX", n=16))
            pb.waits("V", lx + cons_prev)
            sv = []
            for c in range(NCH):
                tv = pb.op("V", lambda E, c=c: E.tensor_add(
                    fl_s[:, 512 * c:512 * (c + 1)], fl_z[:, 512 * c:512 * (c + 1)],
                    fl_xa[:, 512 * c:512 * (c + 1)]), inc="eV")
                pb.wait("V", tv)
                tc_ = pb.op("V", lambda E, c=c: E.tensor_copy(
                    fl_sb[:, 512 * c:512 * (c + 1)], fl_s[:, 512 * c:512 * (c + 1)]), inc="eV")
                sv.append(tc_)
            pb.waits("S", sv)
            pb.waits("T", sv)
            toks_o, _ = ln_stats_apply(
                fl_s, fl_sb, None, None, None, sv, fl_sq, fl_ms, None,
                width=512, fp32_bcast=True, no_gb=True, outf32=fl_o)
            pb.waits("Y", toks_o)
            stt_ = []
            for c in range(NCH):
                ts_ = pb.op("Y", lambda E, b=b, c=c, W0=W0: E.dma_start(
                    out=out[b, 128 * c:128 * (c + 1), W0:W0 + 512],
                    in_=fl_o[:, 512 * c:512 * (c + 1)]), inc="dOut", n=16)
                stt_.append(ts_)
            final_st += stt_
            cons_prev = [pb.tok("eV"), pb.tok("eT"), pb.tok("eS")] + stt_
    pb.op("Y", lambda E: E.wait_ge(pb.sem_h["dOut"], pb.cnt["dOut"]))
    with nc.Block() as block:
        pb.finish(block)
    return nc


def _prep_inputs(inputs, core):
    i = inputs
    x = np.asarray(i["L0_attn_x"])[2 * core:2 * core + 2]
    m = {
        "x_fm": np.ascontiguousarray(np.transpose(x, (0, 2, 1))),
        "wkT": np.ascontiguousarray(np.asarray(i["L0_attn_Wk"]).T),
        "wvT": np.ascontiguousarray(np.asarray(i["L0_attn_Wv"]).T),
        "wrT": np.ascontiguousarray(np.asarray(i["L0_attn_Wr"]).T),
        "woT": np.ascontiguousarray(np.asarray(i["L0_attn_Wo"]).T),
        "wffkT": np.ascontiguousarray(np.asarray(i["L0_attn_Wffk"]).T),
        "wffvT": np.ascontiguousarray(np.asarray(i["L0_attn_Wffv"]).T),
        "ln1g": np.asarray(i["L0_attn_ln1_g"]),
        "ln1b": np.asarray(i["L0_attn_ln1_b"]),
        "ln2g": np.asarray(i["L0_attn_ln2_g"]),
        "ln2b": np.asarray(i["L0_attn_ln2_b"]),
        "td": np.ascontiguousarray(np.asarray(i["L0_attn_time_decay"]).reshape(-1)),
        "tf": np.ascontiguousarray(np.asarray(i["L0_attn_time_first"]).reshape(-1)),
        "wqkv": np.ascontiguousarray(np.asarray(i["L0_ffn_Wqkv"])),
        "wo2": np.ascontiguousarray(np.asarray(i["L0_ffn_Wo"])),
        "w1": np.ascontiguousarray(np.asarray(i["L0_ffn_W1"])),
        "w2": np.ascontiguousarray(np.asarray(i["L0_ffn_W2"])),
        "b1": np.ascontiguousarray(np.asarray(i["L0_ffn_b1"])),
        "b2": np.ascontiguousarray(np.asarray(i["L0_ffn_b2"])),
        "lng": np.ascontiguousarray(np.asarray(i["L0_ffn_ln_g"])),
        "lnb": np.ascontiguousarray(np.asarray(i["L0_ffn_ln_b"])),
        "qs": np.ascontiguousarray(np.asarray(i["L0_ffn_q_scale"])),
        "ks": np.ascontiguousarray(np.asarray(i["L0_ffn_k_scale"])),
    }
    return {k: np.asarray(v, dtype=np.float32) for k, v in m.items()}


def kernel(**inputs):
    nc = build(T=4096, tap=None)
    in_maps = [_prep_inputs(inputs, c) for c in range(8)]
    res = run_bass_kernel_spmd(nc, in_maps, core_ids=list(range(8)))
    shards = []
    for c in range(8):
        o = np.asarray(res.results[c]["out_fm"])
        shards.append(np.transpose(o, (0, 2, 1)))
    return np.ascontiguousarray(np.concatenate(shards, axis=0)).astype(np.float32)


# version 08
# speedup vs baseline: 1.0640x; 1.0640x over previous
import math
import numpy as np

import concourse.bass as bass
import concourse.mybir as mybir
from concourse.bass_utils import run_bass_kernel_spmd

F32 = mybir.dt.float32
BF16 = mybir.dt.bfloat16
I32 = mybir.dt.int32
AF = mybir.ActivationFunctionType
ALU = mybir.AluOpType
AX = mybir.AxisListType

C = 1024
H = 16
NCH = 8
LN_EPS = 1e-5
WKV_EPS = 1e-6
K_CLAMP = 30.0
DEPTH = 6
INNER = 1024
DH = 64
WIN = 256
DI = 2730
D2 = 2 * DI
QK_SCALE = 8.0

TWO_PI = 2.0 * math.pi
_CW1 = np.float32(6.25)
_CW2 = np.float32(np.float64(TWO_PI) - 6.25)
_CW3 = np.float32(np.float64(TWO_PI) - np.float64(_CW1) - np.float64(_CW2))

ENGS = ("T", "V", "S", "G", "Y")
ATTN_PHASES = 9
ATTN_DBG = False


class PB:

    def __init__(self, nc):
        self.nc = nc
        self.th = {e: [] for e in ENGS}
        self.sem_h = {}
        self.cnt = {}
        self.maxw = {}
        self.last = {e: None for e in ENGS}

    def sem(self, name):
        if name not in self.sem_h:
            self.sem_h[name] = self.nc.alloc_semaphore(name)
            self.cnt[name] = 0
        return name

    def tok(self, name):
        return (name, self.cnt[name])

    def wait(self, eng, tok):
        if tok is None:
            return
        name, val = tok
        if name.startswith("d"):
            val = self.cnt[name]
        if val <= 0:
            return
        if not hasattr(self, "wmax"):
            self.wmax = {}
        self.wmax[name] = max(self.wmax.get(name, 0), val)
        key = (eng, name)
        if self.maxw.get(key, 0) >= val:
            return
        self.maxw[key] = val
        h = self.sem_h[name]
        self.th[eng].append(lambda E, h=h, v=val: E.wait_ge(h, v))

    def waits(self, eng, toks):
        for t in toks:
            self.wait(eng, t)

    def op(self, eng, fn, inc=None, n=1):
        if inc is None:
            self.th[eng].append(fn)
            return None
        name = self.sem(inc)
        if name.startswith("d"):
            wv = getattr(self, "wmax", {}).get(name, 0)
            if wv > 0:
                self.wait(eng, (name, wv))
        self.cnt[name] += n
        h = self.sem_h[name]
        self.th[eng].append(lambda E, fn=fn, h=h, n=n: fn(E).then_inc(h, n))
        t = (name, self.cnt[name])
        self.last[eng] = t
        return t

    def finish(self, block):
        block.sync(lambda E: [f(E) for f in self.th["Y"]])
        block.tensor(lambda E: [f(E) for f in self.th["T"]])
        block.vector(lambda E: [f(E) for f in self.th["V"]])
        block.scalar(lambda E: [f(E) for f in self.th["S"]])
        block.gpsimd(lambda E: [f(E) for f in self.th["G"]])


def ceil_div(a, b):
    return (a + b - 1) // b


def build(T=4096, tap=None, skip_pre=False):
    nc = bass.Bass("TRN2", target_bir_lowering=False)
    B = 2
    NT1 = T // 512
    NT2 = T // 256

    dram = {}

    def din(name, shape, dtype=F32):
        dram[name] = nc.dram_tensor(name, list(shape), dtype, kind="ExternalInput").ap()
        return dram[name]

    def dint(name, shape, dtype=F32):
        dram[name] = nc.dram_tensor(name, list(shape), dtype, kind="Internal").ap()
        return dram[name]

    x_in = din("x_fm", (B, C, T))
    wkT = din("wkT", (C, C)); wvT = din("wvT", (C, C))
    wrT = din("wrT", (C, C)); woT = din("woT", (C, C))
    wffkT = din("wffkT", (C, 4 * C)); wffvT = din("wffvT", (4 * C, C))
    ln1g = din("ln1g", (C,)); ln1b = din("ln1b", (C,))
    ln2g = din("ln2g", (C,)); ln2b = din("ln2b", (C,))
    td_in = din("td", (C,)); tf_in = din("tf", (C,))
    wqkv = din("wqkv", (DEPTH, C, 3 * INNER))
    wo2 = din("wo2", (DEPTH, INNER, C))
    w1 = din("w1", (DEPTH, C, D2))
    w2 = din("w2", (DEPTH, DI, C))
    b1_in = din("b1", (DEPTH, D2)); b2_in = din("b2", (DEPTH, C))
    lng_in = din("lng", (DEPTH, C)); lnb_in = din("lnb", (DEPTH, C))
    qs_in = din("qs", (DEPTH, DH)); ks_in = din("ks", (DEPTH, DH))

    scr1 = dint("scr1", (B, C, T))
    scrA = dint("scrA", (B, C, T))
    scrP = dint("scrP", (B, C, T))
    scrQ = dint("scrQ", (B, C, T))
    scr1b = dint("scr1b", (B, C, T), BF16)
    scrAb = dint("scrAb", (B, C, T), BF16)
    scrPb = dint("scrPb", (B, C, T), BF16)
    scrQb = dint("scrQb", (B, C, T), BF16)
    tabs = (dint("tabs", (4, 128, T), BF16) if not ATTN_DBG else
            nc.dram_tensor("tabs", [4, 128, T], BF16, kind="ExternalOutput").ap())

    out = nc.dram_tensor("out_fm", [B, C, T], F32, kind="ExternalOutput").ap()

    pb = PB(nc)

    A = lambda name, shape, dt=F32: nc.alloc_sbuf_tensor(name, list(shape), dt)
    ident_bf = A("ident_bf", [128, 128], BF16)
    r2_bf = A("r2_bf", [128, 128], BF16)
    onescol_f = A("onescol_f", [1, 128], F32)
    onescol_b = A("onescol_b", [1, 128], BF16)
    onessum_b = A("onessum_b", [128, 1], BF16)
    onesrow64 = A("onesrow64", [128, 128], BF16)
    hind = A("hind", [128, 8 * 16], BF16)
    ind16 = A("ind16", [16, 8 * 128], BF16)
    qs8_pp = A("qs8_pp", [128, DEPTH]); ksc_pp = A("ksc_pp", [128, DEPTH])
    iqs8_pp = A("iqs8_pp", [128, DEPTH]); iksc_pp = A("iksc_pp", [128, DEPTH])
    iqs8sq_pp = A("iqs8sq_pp", [128, DEPTH]); iksc2_pp = A("iksc2_pp", [128, DEPTH])
    epsln_pp = A("epsln_pp", [128, 1]); epsw_pp = A("epsw_pp", [128, 1])
    ones512_b = A("ones512_b", [1, 512], BF16)
    negC_pp = A("negC_pp", [128, DEPTH])
    decay_pp = A("decay_pp", [128, NCH]); tf_pp = A("tf_pp", [128, NCH])
    invf_pp = A("invf_pp", [128, 1]); lsc_pp = A("lsc_pp", [128, 1])

    ps6 = nc.alloc_psum_tensor("ps6", [128, 6 * 512], F32)
    psA = nc.alloc_psum_tensor("psA", [128, 512], F32)
    psB = nc.alloc_psum_tensor("psB", [128, 512], F32)
    def bank(i):
        return ps6[:, 512 * i:512 * (i + 1)]

    ARENA_BYTES = (nc.sbuf_bytes_remaining - 512) & ~127
    arena = nc.alloc_sbuf_tensor("arena", [128, ARENA_BYTES], mybir.dt.uint8)
    arena_base = nc.lookup_mloc(arena).addr
    _arena_names = {}

    def at(name, shape, dtype, off):
        key = (name, tuple(shape), dtype, off)
        if key not in _arena_names:
            dt_size = mybir.dt.size(dtype)
            nbytes = int(np.prod(shape[1:])) * dt_size
            assert off + nbytes <= ARENA_BYTES, f"{name}: {off}+{nbytes} > arena"
            _arena_names[key] = nc.alloc_sbuf_tensor_at(
                f"{name}_{off}", list(shape), dtype, offset=arena_base + off)
        return _arena_names[key]

    ii128 = at("ii128", [128, 128], I32, 0)
    itmp = at("itmp", [128, 128], F32, 512)
    teps1 = pb.op("V", lambda E: E.memset(epsln_pp[:, :], LN_EPS), inc="eV")
    tones512 = pb.op("V", lambda E: E.memset(ones512_b[:, :], 1.0), inc="eV")
    teps2 = pb.op("V", lambda E: E.memset(epsw_pp[:, :], WKV_EPS), inc="eV")
    tG = pb.op("G", lambda E: E.iota(ii128[:, :], [[1, 128]], base=0,
                                     channel_multiplier=-1), inc="eG")
    pb.wait("V", tG)
    tI = pb.op("V", lambda E: E.tensor_single_scalar(ident_bf[:, :], ii128[:, :], 0.0, ALU.is_equal), inc="eV")
    c1t = at("c1t", [128, 128], F32, 1024)
    c2t = at("c2t", [128, 128], F32, 1536)
    t1 = pb.op("V", lambda E: E.tensor_single_scalar(c1t[:, :], ii128[:, :], 32.0, ALU.is_equal), inc="eV")
    t2 = pb.op("V", lambda E: E.tensor_single_scalar(c2t[:, :], ii128[:, :], -32.0, ALU.is_equal), inc="eV")
    pb.wait("V", t2)
    for sl in ((0, 32), (64, 96)):
        pb.op("V", lambda E, sl=sl: E.memset(c1t[:, sl[0]:sl[1]], 0.0), inc="eV")
    for sl in ((32, 64), (96, 128)):
        pb.op("V", lambda E, sl=sl: E.memset(c2t[:, sl[0]:sl[1]], 0.0), inc="eV")
    pb.wait("V", pb.tok("eV"))
    tR = pb.op("V", lambda E: E.tensor_sub(r2_bf[:, :], c1t[:, :], c2t[:, :]), inc="eV")

    pb.op("V", lambda E: E.memset(onescol_f[:, :], 1.0), inc="eV")
    pb.op("V", lambda E: E.memset(onescol_b[:, :], 1.0), inc="eV")
    pb.op("V", lambda E: E.memset(onessum_b[:, :], 1.0), inc="eV")
    pb.op("V", lambda E: E.memset(onesrow64[:, :], 0.0), inc="eV")
    tor = pb.tok("eV")
    pb.wait("V", tor)
    pb.op("V", lambda E: E.memset(onesrow64[32:33, :], 1.0), inc="eV")
    pb.op("V", lambda E: E.memset(onesrow64[64:65, :], 1.0), inc="eV")
    pb.op("V", lambda E: E.memset(hind[:, :], 0.0), inc="eV")
    ii16 = at("ii16", [16, 1024], I32, 2048 + 128 * 1024)
    tg16 = pb.op("G", lambda E: E.iota(ii16[:, :].rearrange("p (g r) -> p g r", g=16), [[1, 16], [0, 64]], base=0, channel_multiplier=-1), inc="eG")
    pb.waits("V", [tg16])
    pb.op("V", lambda E: E.tensor_single_scalar(ind16[:, :], ii16[:, :], 0.0, ALU.is_equal), inc="eV")
    th0 = pb.tok("eV")
    pb.wait("V", th0)
    for c in range(NCH):
        pb.op("V", lambda E, c=c: E.memset(hind[0:64, 16 * c + 2 * c:16 * c + 2 * c + 1], 1.0), inc="eV")
        pb.op("V", lambda E, c=c: E.memset(hind[64:128, 16 * c + 2 * c + 1:16 * c + 2 * c + 2], 1.0), inc="eV")


    def pieces_for(k0):
        out_ = []
        for k in range(k0, 6):
            jbr = -256 + 128 * k
            s = max(jbr, 0)
            e = min(jbr + 384, 512)
            out_.append((k, jbr, s, e - s))
        return out_

    def load_pp(dst, src_ap, sem="dC"):
        return pb.op("G", lambda E: E.dma_start(out=dst, in_=src_ap), inc=sem, n=16)

    tdpp_l = []
    for c in range(NCH):
        tdpp_l.append(load_pp(decay_pp[:, c:c + 1], td_in[128 * c:128 * (c + 1)]))
        tdpp_l.append(load_pp(tf_pp[:, c:c + 1], tf_in[128 * c:128 * (c + 1)]))
    pb.waits("S", tdpp_l)
    t_s1 = pb.op("S", lambda E: E.activation(decay_pp[:, :], decay_pp[:, :], AF.Exp), inc="eS")
    pb.wait("S", t_s1)
    t_s2 = pb.op("S", lambda E: E.activation(decay_pp[:, :], decay_pp[:, :], AF.Exp, scale=-1.0), inc="eS")
    pb.wait("S", t_s2)
    t_s3 = pb.op("S", lambda E: E.activation(tf_pp[:, :], tf_pp[:, :], AF.Exp), inc="eS")

    qsrow = at("qsrow", [1, 128], F32, 2048)
    _qk_prev = None
    for l in range(DEPTH):
        pb.waits("G", [_qk_prev])
        l1 = load_pp(qs8_pp[0:64, l:l + 1], qs_in[l, :])
        l2 = load_pp(qs8_pp[64:128, l:l + 1], qs_in[l, :])
        l3 = load_pp(ksc_pp[0:64, l:l + 1], ks_in[l, :])
        l4 = load_pp(ksc_pp[64:128, l:l + 1], ks_in[l, :])
        l5 = load_pp(qsrow[0:1, 0:64], qs_in[l, None, :])
        l6 = load_pp(qsrow[0:1, 64:128], ks_in[l, None, :])
        pb.waits("V", [l5, l6])
        tqk = pb.op("V", lambda E: E.tensor_mul(qsrow[0:1, 0:64], qsrow[0:1, 0:64], qsrow[0:1, 64:128]), inc="eV")
        pb.wait("V", tqk)
        tmx = pb.op("V", lambda E: E.reduce_max(qsrow[0:1, 64:65], qsrow[0:1, 0:64], AX.X), inc="eV")
        pb.wait("V", tmx)
        tc = pb.op("V", lambda E: E.tensor_scalar_mul(qsrow[0:1, 65:66], qsrow[0:1, 64:65], -8.0), inc="eV")
        pb.waits("T", [tc])
        tb = pb.op("T", lambda E: E.matmul(psA[:, 0:1], onescol_f[0:1, :], qsrow[0:1, 65:66], start=True, stop=True), inc="eT")
        pb.waits("V", [tb])
        tv = pb.op("V", lambda E, l=l: E.tensor_copy(negC_pp[:, l:l + 1], psA[:, 0:1]), inc="eV")
        pb.wait("T", tv)
        pb.waits("V", [pb.last["V"]])
        tq8 = pb.op("V", lambda E, l=l: E.tensor_scalar_mul(qs8_pp[:, l:l + 1], qs8_pp[:, l:l + 1], QK_SCALE), inc="eV")
        pb.wait("V", tq8)
        pb.op("V", lambda E, l=l: E.reciprocal(iqs8_pp[:, l:l + 1], qs8_pp[:, l:l + 1]), inc="eV")
        pb.op("V", lambda E, l=l: E.reciprocal(iksc_pp[:, l:l + 1], ksc_pp[:, l:l + 1]), inc="eV")
        pb.waits("V", [pb.last["V"]])
        pb.op("V", lambda E, l=l: E.tensor_mul(iqs8sq_pp[:, l:l + 1], iqs8_pp[:, l:l + 1], iqs8_pp[:, l:l + 1]), inc="eV")
        pb.op("V", lambda E, l=l: E.tensor_mul(iksc2_pp[:, l:l + 1], iksc_pp[:, l:l + 1], iksc_pp[:, l:l + 1]), inc="eV")
        _qk_prev = pb.tok("eV")

    i32 = at("i32", [32, 1], I32, 2560)
    f32t = at("f32t", [32, 1], F32, 2816)
    tg2 = pb.op("G", lambda E: E.iota(i32[:, :], [[1, 1]], base=0, channel_multiplier=1), inc="eG")
    pb.waits("V", [tg2])
    tf32 = pb.op("V", lambda E: E.tensor_copy(f32t[:, :], i32[:, :]), inc="eV")
    pb.waits("V", [tf32])
    tiv = pb.op("V", lambda E: E.tensor_scalar_mul(invf_pp[0:32, :], f32t[:, :], -math.log(10000.0) / 32.0), inc="eV")
    tbs = pb.op("V", lambda E: E.tensor_scalar(lsc_pp[0:32, :], f32t[:, :], 2.0 / 89.6, 25.6 / 89.6, ALU.mult, ALU.add), inc="eV")
    pb.waits("S", [tiv, tbs])
    ts1 = pb.op("S", lambda E: E.activation(invf_pp[0:32, :], invf_pp[0:32, :], AF.Exp), inc="eS")
    ts2 = pb.op("S", lambda E: E.activation(lsc_pp[0:32, :], lsc_pp[0:32, :], AF.Ln), inc="eS")
    pb.waits("Y", [ts1, ts2])
    trep = []
    for bp in (32, 64, 96):
        trep.append(pb.op("Y", lambda E, bp=bp: E.dma_start(out=invf_pp[bp:bp + 32, :], in_=invf_pp[0:32, :]), inc="dT", n=16))
        trep.append(pb.op("Y", lambda E, bp=bp: E.dma_start(out=lsc_pp[bp:bp + 32, :], in_=lsc_pp[0:32, :]), inc="dT", n=16))

    trow = at("trow", [1, T], F32, 3072)
    trowi = at("trowi", [1, T], I32, 3072 + 4 * T)
    for tb0 in range(0, T, 1024):
        _tch = min(1024, T - tb0)
        tg3 = pb.op("G", lambda E, tb0=tb0, _tch=_tch: E.iota(trowi[:, tb0:tb0 + _tch], [[1, _tch]], base=tb0, channel_multiplier=0), inc="eG")
    pb.waits("V", [tg3])
    ttr = pb.op("V", lambda E: E.tensor_copy(trow[:, :], trowi[:, :]), inc="eV")

    TBB = 3072 + 8 * T
    bA = at("tb_A", [128, 512], F32, TBB)
    bB = at("tb_B", [128, 512], F32, TBB + 2048)
    bC_ = at("tb_C", [128, 512], I32, TBB + 4096)
    bD = at("tb_D", [128, 512], F32, TBB + 6144)
    bS1 = at("tb_S1", [128, 512], F32, TBB + 8192)
    bS2 = at("tb_S2", [128, 512], F32, TBB + 10240)
    bH = at("tb_H", [128, 512], F32, TBB + 12288)
    bI = at("tb_I", [128, 512], F32, TBB + 14336)
    bO = at("tb_O", [128, 512], BF16, TBB + 16384)
    pb.waits("T", [ttr] + trep)

    def sin_of(pb, theta_src, out_dst, toks):
        t1 = pb.op("V", lambda E: E.tensor_scalar_mul(bB[:, :], theta_src, 1.0 / TWO_PI), inc="eV")
        pb.wait("V", t1)
        t2 = pb.op("V", lambda E: E.tensor_copy(bC_[:, :], bB[:, :]), inc="eV")
        pb.wait("V", t2)
        t3 = pb.op("V", lambda E: E.tensor_copy(bB[:, :], bC_[:, :]), inc="eV")
        pb.wait("V", t3)
        t4 = None
        for cw in (float(_CW1), float(_CW2), float(_CW3)):
            src_ap = theta_src if t4 is None else out_dst
            t4 = pb.op("V", lambda E, cw=cw, src_ap=src_ap: E.scalar_tensor_tensor(
                out_dst, bB[:, :], -cw, src_ap, ALU.mult, ALU.add), inc="eV")
            pb.wait("V", t4)
        t5a = pb.op("V", lambda E: E.tensor_single_scalar(bB[:, :], out_dst, math.pi, ALU.is_ge), inc="eV")
        pb.wait("V", t5a)
        t5b = pb.op("V", lambda E: E.scalar_tensor_tensor(out_dst, bB[:, :], -TWO_PI, out_dst, ALU.mult, ALU.add), inc="eV")
        pb.wait("V", t5b)
        pb.waits("S", [t5b])
        t6 = pb.op("S", lambda E: E.activation(out_dst, out_dst, AF.Sin), inc="eS")
        return t6

    for cb in range(T // 512):
        c0 = 512 * cb
        tp = pb.op("T", lambda E, c0=c0: E.matmul(psB[:, :], onescol_f[0:1, :], trow[0:1, c0:c0 + 512], start=True, stop=True), inc="eT")
        pb.waits("V", [tp])
        ta = pb.op("V", lambda E: E.tensor_scalar(bA[:, :], psB[:, :], invf_pp[:, 0:1], None, ALU.mult), inc="eV")
        tpw = pb.op("V", lambda E: E.tensor_scalar(bH[:, :], psB[:, :], -float(T // 2), 1.0 / 128.0, ALU.add, ALU.mult), inc="eV")
        pb.wait("V", tpw)
        tpw2 = pb.op("V", lambda E: E.tensor_scalar(bH[:, :], bH[:, :], lsc_pp[:, 0:1], None, ALU.mult), inc="eV")
        pb.wait("T", ta)
        pb.wait("T", tpw)
        pb.wait("V", ta)
        ts_sin = sin_of(pb, bA[:, :], bS1[:, :], [])
        pb.waits("V", [ts_sin])
        tth2 = pb.op("V", lambda E: E.tensor_scalar_add(bA[:, :], bA[:, :], math.pi / 2.0), inc="eV")
        pb.wait("V", tth2)
        ts_cos = sin_of(pb, bA[:, :], bS2[:, :], [])
        pb.waits("S", [tpw2])
        tsc = pb.op("S", lambda E: E.activation(bI[:, :], bH[:, :], AF.Exp), inc="eS")
        pb.wait("S", tsc)
        tisc = pb.op("S", lambda E: E.activation(bH[:, :], bH[:, :], AF.Exp, scale=-1.0), inc="eS")
        combos = [(0, bS2, bI, tsc, ts_cos), (1, bS1, bI, tsc, ts_sin),
                  (2, bS2, bH, tisc, ts_cos), (3, bS1, bH, tisc, ts_sin)]
        for (ti, s_t, sc_t, tk1, tk2) in combos:
            pb.waits("V", [tk1, tk2])
            tv = pb.op("V", lambda E, s_t=s_t, sc_t=sc_t: E.scalar_tensor_tensor(
                bO[:, :], s_t[:, :], 1.0, sc_t[:, :], ALU.mult, ALU.mult), inc="eV")
            pb.waits("Y", [tv])
            tw = pb.op("Y", lambda E, ti=ti, c0=c0: E.dma_start(out=tabs[ti, :, c0:c0 + 512], in_=bO[:, :]), inc="dT", n=16)
            pb.wait("V", tw)

    prologue_toks = [pb.last[e] for e in ("T", "V", "S", "G")] + [pb.tok("dT"), pb.tok("dC")]

    s1_wk = at("s1_wk", [128, NCH * C], BF16, 0)
    s1_wv = at("s1_wv", [128, NCH * C], BF16, 16 * 1024)
    s1_wr = at("s1_wr", [128, NCH * C], BF16, 32 * 1024)
    s1_wo = at("s1_wo", [128, NCH * C], BF16, 48 * 1024)
    s1_x = at("s1_x", [128, NCH * 512], F32, 64 * 1024)
    s1_xb = at("s1_xb", [128, NCH * 512], BF16, 80 * 1024)
    s1_xl = at("s1_xl", [128, NCH * 512], BF16, 88 * 1024)
    s1_w = at("s1_w", [128, NCH * 512], F32, 96 * 1024)
    s1_wv2 = at("s1_wv2", [128, NCH * 512], F32, 112 * 1024)
    s1_sr = at("s1_sr", [128, NCH * 512], BF16, 128 * 1024)
    s1_rk = at("s1_rk", [128, NCH * 512], BF16, 136 * 1024)
    s1_dec = at("s1_dec", [128, NCH * 512], F32, 144 * 1024)
    s1_x2 = at("s1_x2", [128, 512], BF16, 160 * 1024)
    s1_car = at("s1_car", [128, 2 * NCH], F32, 161 * 1024)
    s1_ms = at("s1_ms", [1, 512 * 4], F32, 163 * 1024)
    s1_x1 = at("s1_x1", [128, NCH * 512], F32, 172 * 1024)
    gb_rw1 = at("gb_rw1", [2, C], BF16, 188 * 1024)
    g1pp = at("g1pp", [128, NCH], F32, 190 * 1024)
    b1pp_rw = at("b1pp_rw", [128, NCH], F32, 190 * 1024 + 64)

    pb.waits("G", prologue_toks)
    wload = []
    for c in range(NCH):
        wload.append(load_pp(g1pp[:, c:c + 1], ln1g[128 * c:128 * (c + 1)]))
        wload.append(load_pp(b1pp_rw[:, c:c + 1], ln1b[128 * c:128 * (c + 1)]))
    wload.append(load_pp(gb_rw1[0:1, :], ln1g[None, :]))
    wload.append(load_pp(gb_rw1[1:2, :], ln1b[None, :]))
    for (dst, src) in ((s1_wk, wkT), (s1_wv, wvT), (s1_wr, wrT), (s1_wo, woT)):
        for c in range(NCH):
            wload.append(pb.op("G", lambda E, dst=dst, src=src, c=c: E.dma_start(
                out=dst[:, C * c:C * (c + 1)], in_=src[128 * c:128 * (c + 1), :]), inc="dW", n=16))
    tmem = pb.op("V", lambda E: E.memset(s1_dec[:, :], 1.0), inc="eV")
    pb.wait("V", tmem)
    for c in range(NCH):
        pb.op("V", lambda E, c=c: E.tensor_scalar(
            s1_dec[:, 512 * c:512 * (c + 1)], s1_dec[:, 512 * c:512 * (c + 1)],
            decay_pp[:, c:c + 1], None, ALU.mult), inc="eV")
    tdec = pb.tok("eV")

    def ln_stats_apply(xf, xbf, xlout, gb2ap, gpp_ap, dep_toks, sq, msr, scr,
                       width=512, fp32_bcast=False, no_gb=False, outf32=None,
                       bpp_ap=None, sq_eng="G"):
        pb.waits("T", dep_toks)
        tk = None
        for c in range(NCH):
            tk = pb.op("T", lambda E, c=c: E.matmul(
                psA[0:1, 0:width], onessum_b[:, 0:1], xbf[:, width * c:width * c + width],
                start=(c == 0), stop=(c == NCH - 1)),
                inc=("eT" if c == NCH - 1 else None))
        t_sum = tk
        ginc = "eG" if sq_eng == "G" else "eS"
        pb.waits(sq_eng, dep_toks)
        if xlout is not None:
            slot_of = lambda c: xlout[:, width * c:width * c + width]
            sq_toks = []
            for c in range(NCH):
                tq = pb.op(sq_eng, lambda E, c=c: E.tensor_mul(
                    slot_of(c), xbf[:, width * c:width * c + width],
                    xbf[:, width * c:width * c + width]), inc=ginc)
                sq_toks.append(tq)
            t_sq_mm = None
            for c in range(NCH):
                pb.waits("T", [sq_toks[c]])
                t_sq_mm = pb.op("T", lambda E, c=c: E.matmul(
                    psB[0:1, 0:width], onessum_b[:, 0:1], slot_of(c),
                    start=(c == 0), stop=(c == NCH - 1)), inc="eT")
        else:
            t_sq_mm = None
            for c in range(NCH):
                pb.waits(sq_eng, [t_sq_mm])
                tq = pb.op(sq_eng, lambda E, c=c: E.tensor_mul(
                    sq[:, 0:width], xbf[:, width * c:width * c + width],
                    xbf[:, width * c:width * c + width]), inc=ginc)
                pb.waits("T", [tq])
                t_sq_mm = pb.op("T", lambda E, c=c: E.matmul(
                    psB[0:1, 0:width], onessum_b[:, 0:1], sq[:, 0:width],
                    start=(c == 0), stop=(c == NCH - 1)), inc="eT")
        pb.waits("V", [t_sum, t_sq_mm])
        tm = pb.op("V", lambda E: E.tensor_scalar_mul(msr[0:1, 0:width], psA[0:1, 0:width], 1.0 / C), inc="eV")
        pb.wait("V", tm)
        tms = pb.op("V", lambda E: E.tensor_mul(msr[0:1, width:2 * width], msr[0:1, 0:width], msr[0:1, 0:width]), inc="eV")
        pb.wait("V", tms)
        tvr = pb.op("V", lambda E: E.scalar_tensor_tensor(
            msr[0:1, width:2 * width], psB[0:1, 0:width], 1.0 / C, msr[0:1, width:2 * width],
            ALU.mult, ALU.subtract), inc="eV")
        pb.waits("S", [tvr])
        tln = pb.op("S", lambda E: E.activation(msr[0:1, 2 * width:3 * width], msr[0:1, width:2 * width], AF.Ln, bias=epsln_pp[0:1, 0:1]), inc="eS")
        pb.wait("S", tln)
        trs = pb.op("S", lambda E: E.activation(msr[0:1, 2 * width:3 * width], msr[0:1, 2 * width:3 * width], AF.Exp, scale=-0.5), inc="eS")
        if fp32_bcast:
            pb.waits("T", [trs, tvr, tm])
            t_rb = pb.op("T", lambda E: E.matmul(psA[:, 0:width], onescol_f[0:1, :], msr[0:1, 2 * width:3 * width], start=True, stop=True), inc="eT")
            t_mb = pb.op("T", lambda E: E.matmul(psB[:, 0:width], onescol_f[0:1, :], msr[0:1, 0:width], start=True, stop=True), inc="eT")
        else:
            rstd_row, cb_rows = scr
            pb.waits("V", [trs])
            trsb = pb.op("V", lambda E: E.tensor_copy(rstd_row[0:1, 0:width], msr[0:1, 2 * width:3 * width]), inc="eV")
            tmb_ = pb.op("V", lambda E: E.tensor_copy(cb_rows[0:1, 0:width], msr[0:1, 0:width]), inc="eV")
            pb.waits("T", [trsb, tmb_, tvr, tm])
            t_rb = pb.op("T", lambda E: E.matmul(psA[:, 0:width], onescol_b[0:1, :], rstd_row[0:1, 0:width], start=True, stop=True), inc="eT")
            t_mb = pb.op("T", lambda E: E.matmul(psB[:, 0:width], onescol_b[0:1, :], cb_rows[0:1, 0:width], start=True, stop=True), inc="eT")
        toks_c = []
        for c in range(NCH):
            dst = xlout if not no_gb else (outf32 if outf32 is not None else xlout)
            pb.waits("V", [t_rb, t_mb])
            tv1 = pb.op("V", lambda E, c=c, dst=dst: E.tensor_tensor(
                dst[:, width * c:width * c + width], xf[:, width * c:width * c + width],
                psB[:, 0:width], ALU.subtract), inc="eV")
            pb.wait("V", tv1)
            if no_gb:
                tv2 = pb.op("V", lambda E, c=c, dst=dst: E.tensor_tensor(
                    dst[:, width * c:width * c + width], dst[:, width * c:width * c + width],
                    psA[:, 0:width], ALU.mult), inc="eV")
                toks_c.append(tv2)
                if outf32 is not None and xlout is not None:
                    pb.wait("V", tv2)
                    tv3 = pb.op("V", lambda E, c=c: E.tensor_copy(
                        xlout[:, width * c:width * c + width], outf32[:, width * c:width * c + width]), inc="eV")
                    toks_c.append(tv3)
            else:
                tv2 = pb.op("V", lambda E, c=c, dst=dst: E.scalar_tensor_tensor(
                    dst[:, width * c:width * c + width], dst[:, width * c:width * c + width],
                    gpp_ap[:, c:c + 1], psA[:, 0:width], ALU.mult, ALU.mult), inc="eV")
                pb.wait("V", tv2)
                tv3 = pb.op("V", lambda E, c=c, dst=dst: E.tensor_scalar(
                    dst[:, width * c:width * c + width], dst[:, width * c:width * c + width],
                    bpp_ap[:, c:c + 1], None, ALU.add), inc="eV")
                toks_c.append(tv3)
        return toks_c, t_rb

    if skip_pre:
        globals()
    st_s1 = []
    prev_x1_cons = {}
    s1_stb_prev = {}
    pb.waits("V", prologue_toks)
    pb.waits("S", prologue_toks)
    pb.waits("T", prologue_toks)
    last_store = []
    s1_rstd = at("s1_rstd", [1, 512], BF16, 162 * 1024)
    s1_cbr = at("s1_cbr", [2, 512], BF16, 162 * 1024 + 1024 + 128)
    s1_x1b = at("s1_x1b", [128, 2 * 512], BF16, 191 * 1024)
    pb.waits("Y", [tones512])
    ln_scr_prep = pb.op("Y", lambda E: E.dma_start(out=s1_cbr[1:2, :], in_=ones512_b[0:1, :]), inc="dO1", n=16)

    for b in range(B if not skip_pre else 0):
        for t in range(NT1):
            t0 = 512 * t
            lx = []
            lxb = []
            for c in range(NCH):
                pb.waits("Y", prev_x1_cons.get(("x", c), []))
                lx.append(pb.op("Y", lambda E, b=b, c=c, t0=t0: E.dma_start(
                    out=s1_x[:, 512 * c:512 * (c + 1)], in_=x_in[b, 128 * c:128 * (c + 1), t0:t0 + 512]), inc="dX", n=16))
                pb.waits("G", prev_x1_cons.get(("xb", c), []))
                lxb.append(pb.op("G", lambda E, b=b, c=c, t0=t0: E.dma_start(
                    out=s1_xb[:, 512 * c:512 * (c + 1)], in_=x_in[b, 128 * c:128 * (c + 1), t0:t0 + 512]), inc="dXb", n=16))
            pb.waits("S", lxb)
            pb.waits("V", lx + lxb + [ln_scr_prep] + wload)
            toks_xl, t_rb = ln_stats_apply(
                s1_x, s1_xb, s1_xl, gb_rw1, g1pp, lxb, s1_x2, s1_ms, (s1_rstd, s1_cbr),
                bpp_ap=b1pp_rw)
            prev_x1_cons[("xb", 0)] = [pb.tok("eT")]
            wkv_toks = []
            sr_toks = []
            for c in range(NCH):
                pb.waits("T", toks_xl + wload)
                for (wsb, bk) in ((s1_wk, 0), (s1_wv, 1), (s1_wr, 2)):
                    for ci in range(NCH):
                        tk = pb.op("T", lambda E, wsb=wsb, bk=bk, c=c, ci=ci: E.matmul(
                            bank(bk), wsb[:, C * ci + 128 * c:C * ci + 128 * (c + 1)],
                            s1_xl[:, 512 * ci:512 * (ci + 1)],
                            start=(ci == 0), stop=(ci == NCH - 1)),
                            inc=("eT" if ci == NCH - 1 else None))
                    if wsb is s1_wk:
                        t_k = tk
                    elif wsb is s1_wv:
                        t_v = tk
                    else:
                        t_r = tk
                pb.waits("V", [t_k])
                tw1 = pb.op("V", lambda E, c=c: E.tensor_scalar_min(
                    s1_w[:, 512 * c:512 * (c + 1)], bank(0), K_CLAMP), inc="eV")
                pb.waits("S", [tw1])
                tw2 = pb.op("S", lambda E, c=c: E.activation(
                    s1_w[:, 512 * c:512 * (c + 1)], s1_w[:, 512 * c:512 * (c + 1)], AF.Exp), inc="eS")
                pb.waits("V", [t_v, tw2])
                tw3 = pb.op("V", lambda E, c=c: E.tensor_tensor(
                    s1_wv2[:, 512 * c:512 * (c + 1)], s1_w[:, 512 * c:512 * (c + 1)], bank(1), ALU.mult), inc="eV")
                pb.waits("S", [t_r])
                tw4 = pb.op("S", lambda E, c=c: E.activation(
                    s1_sr[:, 512 * c:512 * (c + 1)], bank(2), AF.Sigmoid), inc="eS")
                sr_toks.append(tw4)
                pb.waits("T", [tw1, tw3, tw4])
                wkv_toks.append((c, tw2, tw3))
            scan_toks = []
            for (c, tw2, tw3) in wkv_toks:
                if t == 0:
                    pb.waits("V", [tw2])
                    ta1 = pb.op("V", lambda E, c=c: E.tensor_scalar(
                        s1_w[:, 512 * c:512 * c + 1], s1_w[:, 512 * c:512 * c + 1],
                        tf_pp[:, c:c + 1], None, ALU.mult), inc="eV")
                    ta2 = pb.op("V", lambda E, c=c: E.tensor_scalar(
                        s1_wv2[:, 512 * c:512 * c + 1], s1_wv2[:, 512 * c:512 * c + 1],
                        tf_pp[:, c:c + 1], None, ALU.mult), inc="eV")
                    tz = pb.op("V", lambda E, c=c: E.memset(s1_car[:, 2 * c:2 * c + 2], 0.0), inc="eV")
                    pb.waits("V", [ta1, ta2, tz])
                pb.waits("V", [tw2, tw3, tdec])
                ts1_ = pb.op("V", lambda E, c=c: E.tensor_tensor_scan(
                    s1_wv2[:, 512 * c:512 * (c + 1)], s1_dec[:, 512 * c:512 * (c + 1)],
                    s1_wv2[:, 512 * c:512 * (c + 1)], s1_car[:, 2 * c:2 * c + 1], ALU.mult, ALU.add), inc="eV")
                pb.wait("V", ts1_)
                ts2_ = pb.op("V", lambda E, c=c: E.tensor_tensor_scan(
                    s1_w[:, 512 * c:512 * (c + 1)], s1_dec[:, 512 * c:512 * (c + 1)],
                    s1_w[:, 512 * c:512 * (c + 1)], s1_car[:, 2 * c + 1:2 * c + 2], ALU.mult, ALU.add), inc="eV")
                pb.wait("V", ts2_)
                tc1 = pb.op("V", lambda E, c=c: E.tensor_copy(s1_car[:, 2 * c:2 * c + 1], s1_wv2[:, 512 * c + 511:512 * c + 512]), inc="eV")
                tc2 = pb.op("V", lambda E, c=c: E.tensor_copy(s1_car[:, 2 * c + 1:2 * c + 2], s1_w[:, 512 * c + 511:512 * c + 512]), inc="eV")
                pb.waits("V", [tc1, tc2])
                pb.waits("S", [ts2_, tc2])
                td1 = pb.op("S", lambda E, c=c: E.activation(
                    s1_w[:, 512 * c:512 * (c + 1)], s1_w[:, 512 * c:512 * (c + 1)], AF.Ln, bias=epsw_pp[:, 0:1]), inc="eS")
                pb.wait("S", td1)
                td2 = pb.op("S", lambda E, c=c: E.activation(
                    s1_w[:, 512 * c:512 * (c + 1)], s1_w[:, 512 * c:512 * (c + 1)], AF.Exp, scale=-1.0), inc="eS")
                pb.waits("V", [td2, tc1])
                td3 = pb.op("V", lambda E, c=c: E.tensor_tensor(
                    s1_wv2[:, 512 * c:512 * (c + 1)], s1_wv2[:, 512 * c:512 * (c + 1)],
                    s1_w[:, 512 * c:512 * (c + 1)], ALU.mult), inc="eV")
                pb.wait("V", td3)
                td4 = pb.op("V", lambda E, c=c: E.tensor_tensor(
                    s1_rk[:, 512 * c:512 * (c + 1)], s1_sr[:, 512 * c:512 * (c + 1)],
                    s1_wv2[:, 512 * c:512 * (c + 1)], ALU.mult), inc="eV")
                scan_toks.append(td4)
            pb.waits("T", scan_toks)
            pb.waits("T", last_store)
            st_tile = []
            for c in range(NCH):
                for ci in range(NCH):
                    tk = pb.op("T", lambda E, c=c, ci=ci: E.matmul(
                        bank(3), s1_wo[:, C * ci + 128 * c:C * ci + 128 * (c + 1)],
                        s1_rk[:, 512 * ci:512 * (ci + 1)],
                        start=(ci == 0), stop=(ci == NCH - 1)),
                        inc=("eT" if ci == NCH - 1 else None))
                pb.waits("V", [tk])
                te = pb.op("V", lambda E, c=c: E.scalar_tensor_tensor(
                    s1_x1[:, 512 * c:512 * (c + 1)], bank(3), 1.0,
                    s1_x[:, 512 * c:512 * (c + 1)], ALU.mult, ALU.add), inc="eV")
                pb.waits("V", [s1_stb_prev.get(c % 2), te])
                tcb = pb.op("V", lambda E, c=c: E.tensor_copy(
                    s1_x1b[:, 512 * (c % 2):512 * (c % 2) + 512],
                    s1_x1[:, 512 * c:512 * (c + 1)]), inc="eV")
                pb.waits("T", [te])
                pb.waits("Y", [te, tcb])
                ts_ = pb.op("Y", lambda E, b=b, c=c, t0=t0: E.dma_start(
                    out=scr1[b, 128 * c:128 * (c + 1), t0:t0 + 512],
                    in_=s1_x1[:, 512 * c:512 * (c + 1)]), inc="dS1", n=16)
                tsb_ = pb.op("Y", lambda E, b=b, c=c, t0=t0: E.dma_start(
                    out=scr1b[b, 128 * c:128 * (c + 1), t0:t0 + 512],
                    in_=s1_x1b[:, 512 * (c % 2):512 * (c % 2) + 512]), inc="dS1", n=16)
                s1_stb_prev[c % 2] = tsb_
                st_tile.append(ts_)
                st_tile.append(tsb_)
            last_store = st_tile
            st_s1.extend(st_tile)
            prev_x1_cons[("x", 0)] = [pb.tok("eV")]
            prev_x1_cons[("xb", 0)] = [pb.tok("eT"), pb.tok("eS")]
            for c in range(NCH):
                prev_x1_cons[("x", c)] = prev_x1_cons[("x", 0)]
                prev_x1_cons[("xb", c)] = prev_x1_cons[("xb", 0)]

    def stage_barrier_toks():
        return ([pb.last[e] for e in ("T", "V", "S", "G") if pb.last[e] is not None]
                + [pb.tok(s) for s in list(pb.cnt) if s.startswith("d")])

    def emit_tap(scr):
        tb_buf = at("tapbuf", [128, 512], F32, 0)
        for e in ENGS:
            pb.waits(e, stage_barrier_toks())
        prev = []
        for b in range(B):
            for t in range(NT1):
                for c in range(NCH):
                    pb.waits("Y", prev)
                    l1 = pb.op("Y", lambda E, b=b, c=c, t=t: E.dma_start(
                        out=tb_buf[:, :], in_=scr[b, 128 * c:128 * (c + 1), 512 * t:512 * (t + 1)]), inc="dTap", n=16)
                    pb.waits("Y", [l1])
                    s1_ = pb.op("Y", lambda E, b=b, c=c, t=t: E.dma_start(
                        out=out[b, 128 * c:128 * (c + 1), 512 * t:512 * (t + 1)], in_=tb_buf[:, :]), inc="dOut", n=16)
                    prev = [s1_]
        pb.op("Y", lambda E: E.wait_ge(pb.sem_h["dOut"], pb.cnt["dOut"]))
        with nc.Block() as block:
            pb.finish(block)
        return nc

    if tap == "scr1":
        return emit_tap(scr1)

    W2T = 256
    s2_wk = at("s2_wk", [128, NCH * 4096], BF16, 0)
    s2_wv = at("s2_wv", [128, 32 * 1024], BF16, 64 * 1024)
    s2_x1 = at("s2_x1", [128, NCH * W2T], F32, 128 * 1024)
    s2_x1b = at("s2_x1b", [128, NCH * W2T], BF16, 136 * 1024)
    s2_xl2 = at("s2_xl2", [128, NCH * W2T], BF16, 140 * 1024)
    s2_h = at("s2_h", [128, 32 * W2T], BF16, 144 * 1024)
    s2_x = at("s2_x", [128, NCH * W2T], F32, 160 * 1024)
    s2_s = at("s2_s", [128, NCH * W2T], F32, 168 * 1024)
    s2_sb = at("s2_sb", [128, NCH * W2T], BF16, 176 * 1024)
    s2_sq = at("s2_sq", [128, W2T], BF16, 180 * 1024)
    s2_ms = at("s2_ms", [1, 4 * W2T], F32, 181 * 1024)
    s2_rstd = at("s2_rstd", [1, W2T], BF16, 185 * 1024)
    s2_cbr = at("s2_cbr", [2, W2T], BF16, 185 * 1024 + 640)
    s2_xa = at("s2_xa", [128, NCH * W2T], F32, 187 * 1024)
    gb_rw2 = at("gb_rw2", [2, C], BF16, 196 * 1024)
    g2pp = at("g2pp", [128, NCH], F32, 198 * 1024 + 128)
    b2pp_rw = at("b2pp_rw", [128, NCH], F32, 198 * 1024 + 256)

    s1_done = stage_barrier_toks()
    pb.waits("G", s1_done)
    pb.waits("S", s1_done)
    w2load = []
    for ci in range(NCH):
        w2load.append(pb.op("G", lambda E, ci=ci: E.dma_start(
            out=s2_wk[:, 4096 * ci:4096 * (ci + 1)], in_=wffkT[128 * ci:128 * (ci + 1), :]), inc="dW", n=16))
    for j in range(32):
        w2load.append(pb.op("G", lambda E, j=j: E.dma_start(
            out=s2_wv[:, 1024 * j:1024 * (j + 1)], in_=wffvT[128 * j:128 * (j + 1), :]), inc="dW", n=16))
    for c in range(NCH):
        w2load.append(load_pp(g2pp[:, c:c + 1], ln2g[128 * c:128 * (c + 1)]))
        w2load.append(load_pp(b2pp_rw[:, c:c + 1], ln2b[128 * c:128 * (c + 1)]))
    w2load.append(load_pp(gb_rw2[0:1, :], ln2g[None, :]))
    w2load.append(load_pp(gb_rw2[1:2, :], ln2b[None, :]))
    pb.waits("Y", s1_done + [tones512])
    tcb2 = pb.op("Y", lambda E: E.dma_start(out=s2_cbr[1:2, :], in_=ones512_b[0:1, 0:W2T]), inc="dO1", n=16)
    w2load.append(tcb2)

    st_s2 = []
    s2_prev_cons = []
    s2_xl2_prev = []
    s2_psum_prev = []
    s2_stb_prev = []
    s2_tiles = [(b, t) for b in range(B if not skip_pre else 0) for t in range(NT2)]

    def emit_s2_loads(b, t):
        t0 = W2T * t
        lx = []
        pb.waits("Y", s2_prev_cons)
        pb.waits("S", s2_prev_cons)
        for c in range(NCH):
            lx.append(pb.op("Y", lambda E, b=b, c=c, t0=t0: E.dma_start(
                out=s2_x1[:, W2T * c:W2T * (c + 1)], in_=scr1[b, 128 * c:128 * (c + 1), t0:t0 + W2T]), inc="dX", n=16))
            lx.append(pb.op("S", lambda E, b=b, c=c, t0=t0: E.dma_start(
                out=s2_x1b[:, W2T * c:W2T * (c + 1)], in_=scr1b[b, 128 * c:128 * (c + 1), t0:t0 + W2T]), inc="dXc", n=16))
            lx.append(pb.op("Y", lambda E, b=b, c=c, t0=t0: E.dma_start(
                out=s2_x[:, W2T * c:W2T * (c + 1)], in_=x_in[b, 128 * c:128 * (c + 1), t0:t0 + W2T]), inc="dX", n=16))
        return lx

    s2_pend = {}
    if s2_tiles:
        s2_pend[0] = emit_s2_loads(*s2_tiles[0])
    for s2_ti, (b, t) in enumerate(s2_tiles):
        if True:
            t0 = W2T * t
            lx = s2_pend.pop(s2_ti)
            pb.waits("G", s2_prev_cons + s2_xl2_prev)
            pb.waits("V", lx + w2load)
            pb.waits("S", lx)
            pb.waits("T", lx + w2load)
            toks_xl2, t_rb_ln2 = ln_stats_apply(
                s2_x1, s2_x1b, s2_xl2, gb_rw2, g2pp, lx + s2_psum_prev, s2_sq, s2_ms,
                (s2_rstd, s2_cbr), width=W2T, bpp_ap=b2pp_rw)
            tg_ln2 = pb.tok("eG")
            txx = []
            for c in range(NCH):
                txx.append(pb.op("V", lambda E, c=c: E.tensor_add(
                    s2_s[:, W2T * c:W2T * (c + 1)], s2_x[:, W2T * c:W2T * (c + 1)],
                    s2_x1[:, W2T * c:W2T * (c + 1)]), inc="eV"))
            s2_prev_cons = toks_xl2 + txx + [t_rb_ln2, tg_ln2]
            if s2_ti + 1 < len(s2_tiles):
                s2_pend[s2_ti + 1] = emit_s2_loads(*s2_tiles[s2_ti + 1])
            h_toks = []
            pb.waits("T", toks_xl2)
            s2_xl2_gate = []
            for j in range(32):
                bk = j % 4
                for ci in range(NCH):
                    tk = pb.op("T", lambda E, j=j, ci=ci, bk=bk: E.matmul(
                        bank(bk)[:, 0:W2T], s2_wk[:, 4096 * ci + 128 * j:4096 * ci + 128 * (j + 1)],
                        s2_xl2[:, W2T * ci:W2T * (ci + 1)],
                        start=(ci == 0), stop=(ci == NCH - 1)),
                        inc=("eT" if ci == NCH - 1 else None))
                pb.waits("S", [tk])
                tr = pb.op("S", lambda E, j=j, bk=bk: E.activation(
                    s2_h[:, W2T * j:W2T * (j + 1)], bank(bk)[:, 0:W2T], AF.Relu), inc="eS")
                pb.waits("V", [tr])
                tq = pb.op("V", lambda E, j=j: E.tensor_tensor(
                    s2_h[:, W2T * j:W2T * (j + 1)], s2_h[:, W2T * j:W2T * (j + 1)],
                    s2_h[:, W2T * j:W2T * (j + 1)], ALU.mult), inc="eV")
                h_toks.append(tq)
                s2_xl2_gate = [tk]
                pb.waits("T", [tr])
            pb.waits("T", h_toks)
            pb.waits("V", s2_stb_prev)
            s_toks = []
            for c in range(NCH):
                bk = 4 + (c % 2)
                for j in range(32):
                    tk = pb.op("T", lambda E, c=c, j=j, bk=bk: E.matmul(
                        bank(bk)[:, 0:W2T], s2_wv[:, 1024 * j + 128 * c:1024 * j + 128 * (c + 1)],
                        s2_h[:, W2T * j:W2T * (j + 1)],
                        start=(j == 0), stop=(j == 31)),
                        inc=("eT" if j == 31 else None))
                pb.waits("V", [tk] + txx)
                te = pb.op("V", lambda E, c=c, bk=bk: E.scalar_tensor_tensor(
                    s2_s[:, W2T * c:W2T * (c + 1)], bank(bk)[:, 0:W2T], 1.0,
                    s2_s[:, W2T * c:W2T * (c + 1)], ALU.mult, ALU.add), inc="eV")
                s_toks.append(te)
                pb.waits("T", [te])
                pb.wait("V", te)
                tcst = pb.op("V", lambda E, c=c: E.tensor_copy(
                    s2_sb[:, W2T * c:W2T * (c + 1)], s2_s[:, W2T * c:W2T * (c + 1)]), inc="eV")
                s_toks.append(tcst)
            pb.waits("S", s_toks)
            pb.waits("V", s_toks)
            toks_xa, t_rb_xa = ln_stats_apply(
                s2_s, s2_sb, None, None, None, s_toks, s2_sq, s2_ms,
                None, width=W2T, fp32_bcast=True, no_gb=True, outf32=s2_xa)
            st_tile = []
            pb.waits("V", [t_rb_xa, pb.tok("eG")] + toks_xa)
            tcbs = []
            for c in range(NCH):
                tcbs.append(pb.op("V", lambda E, c=c: E.tensor_copy(
                    s2_sb[:, W2T * c:W2T * (c + 1)], s2_xa[:, W2T * c:W2T * (c + 1)]), inc="eV"))
            pb.waits("Y", toks_xa + tcbs)
            for c in range(NCH):
                ts_ = pb.op("Y", lambda E, b=b, c=c, t0=t0: E.dma_start(
                    out=scrA[b, 128 * c:128 * (c + 1), t0:t0 + W2T],
                    in_=s2_xa[:, W2T * c:W2T * (c + 1)]), inc="dS2", n=16)
                tsb_ = pb.op("Y", lambda E, b=b, c=c, t0=t0: E.dma_start(
                    out=scrAb[b, 128 * c:128 * (c + 1), t0:t0 + W2T],
                    in_=s2_sb[:, W2T * c:W2T * (c + 1)]), inc="dS2", n=16)
                st_tile.append(ts_)
                st_tile.append(tsb_)
            st_s2.extend(st_tile)
            s2_xl2_prev = s2_xl2_gate
            s2_psum_prev = toks_xa
            s2_stb_prev = st_tile

    if tap == "scrA":
        return emit_tap(scrA)

    NST = T // 512

    def emit_attn_stage(l, src, srcb, dst, dstb):
        a_wqkv = at("a_wqkv", [128, NCH * 3072], BF16, 0)
        a_wo = at("a_wo", [128, NCH * 1024], BF16, 49152)
        a_xs = at("a_xs", [128, 2 * NCH * 512], BF16, 65536)
        a_qr = at("a_qr", [128, NCH * 512], BF16, 81920)
        a_kr = at("a_kr", [128, NCH * 512], BF16, 90112)
        a_q2 = at("a_q2", [128, 16 * 512], BF16, 98304)
        a_qf = at("a_qf", [128, NCH * 512], BF16, 114688)
        a_kroll = at("a_kroll", [128, NCH * 768], BF16, 122880)
        a_vroll = at("a_vroll", [128, 6 * 1280], BF16, 135168)
        a_e = at("a_e", [128, 2 * 3072], BF16, 150528)
        a_rnq = at("a_rnq", [16, 512], F32, 162816)
        a_rnk = at("a_rnk", [16, 512], F32, 164864)
        a_rnqb = at("a_rnqb", [32, 512], BF16, 166912)
        a_rnkb = at("a_rnkb", [32, 512], BF16, 167936)
        a_rnsb = at("a_rnsb", [128, 2 * 512], BF16, 168960)
        a_rs = at("a_rs", [128, 512], F32, 162816)
        a_rsb = at("a_rsb", [128, 512], BF16, 164864)
        a_w1b = at("a_w1b", [128, 2 * 512], BF16, 171008)
        a_w2b = at("a_w2b", [128, 2 * 512], BF16, 173056)
        a_ao = at("a_ao", [128, NCH * 512], BF16, 175104)
        a_xres = at("a_xres", [128, 2 * 512], F32, 183296)
        a_xnew = at("a_xnew", [128, 2 * 512], F32, 187392)
        a_tab = at("a_tab", [128, 2 * 4 * 512], BF16, 191488)
        a_mask = at("a_mask", [128, 6 * 512], BF16, 199680)
        a_xnb = at("a_xnb", [128, 2 * 512], BF16, 168960)
        a_hsc = at("a_hsc", [128, 2 * 128], BF16, 205824)
        prev = stage_barrier_toks()
        pb.waits("G", prev)
        pb.waits("V", prev)
        pb.waits("Y", prev)
        pb.waits("S", prev)
        th1 = pb.op("V", lambda E, l=l: E.tensor_scalar(
            a_hsc[:, 0:128], hind[:, :], iqs8sq_pp[:, l:l + 1], None, ALU.mult), inc="eV")
        th2 = pb.op("V", lambda E, l=l: E.tensor_scalar(
            a_hsc[:, 128:256], hind[:, :], iksc2_pp[:, l:l + 1], None, ALU.mult), inc="eV")
        hsc_toks = [th1, th2]
        wl = []
        for ci in range(NCH):
            wl.append(pb.op("G", lambda E, ci=ci: E.dma_start(
                out=a_wqkv[:, 3072 * ci:3072 * (ci + 1)], in_=wqkv[l, 128 * ci:128 * (ci + 1), :]), inc="dW", n=16))
            wl.append(pb.op("G", lambda E, ci=ci: E.dma_start(
                out=a_wo[:, 1024 * ci:1024 * (ci + 1)], in_=wo2[l, 128 * ci:128 * (ci + 1), :]), inc="dW", n=16))
        tmsk = pb.op("G", lambda E: E.memset(a_mask[:, :], 1.0), inc="eG")
        pb.wait("G", tmsk)
        for k in range(6):
            jbr = -256 + 128 * k
            col0 = 512 * k
            t2_ = pb.op("G", lambda E, col0=col0, jbr=jbr: E.affine_select(
                a_mask[:, col0:col0 + 512], a_mask[:, col0:col0 + 512],
                [[1, 512]], ALU.is_ge, 0.0, base=-jbr, channel_multiplier=-1), inc="eG")
            pb.wait("G", t2_)
            t3_ = pb.op("G", lambda E, col0=col0, jbr=jbr: E.affine_select(
                a_mask[:, col0:col0 + 512], a_mask[:, col0:col0 + 512],
                [[-1, 512]], ALU.is_ge, 0.0, base=jbr + 256, channel_multiplier=1), inc="eG")
            pb.wait("G", t3_)
        wl.append(pb.tok("eG"))
        tz = []
        tz.append(pb.op("V", lambda E: E.memset(a_rnqb[:, :], 0.0), inc="eV"))
        tz.append(pb.op("V", lambda E: E.memset(a_rnkb[:, :], 0.0), inc="eV"))
        for par in range(2):
            for k in (3, 4, 5):
                sr = 128 * (k - 2)
                tz.append(pb.op("V", lambda E, par=par, k=k, sr=sr: E.memset(
                    a_e[:, 3072 * par + 512 * k:3072 * par + 512 * k + sr], 0.0), inc="eV"))
            for k in (0, 1, 2):
                en = 128 * (k + 1)
                tz.append(pb.op("V", lambda E, par=par, k=k, en=en: E.memset(
                    a_e[:, 3072 * par + 512 * k + en:3072 * par + 512 * (k + 1)], 0.0), inc="eV"))
        for sl in range(6):
            for p in range(NCH):
                tz.append(pb.op("V", lambda E, sl=sl, p=p: E.memset(
                    a_vroll[:, 1280 * sl + 160 * p + 64:1280 * sl + 160 * p + 65], 1.0), inc="eV"))
                tz.append(pb.op("V", lambda E, sl=sl, p=p: E.memset(
                    a_vroll[:, 1280 * sl + 160 * p + 65:1280 * sl + 160 * p + 96], 0.0), inc="eV"))
        wl += tz

        store_toks = []
        kv_readers = {}
        xs_cons = {0: [], 1: []}
        tab_cons = {0: [], 1: []}
        e_pv = {0: None, 1: None}
        stats_prev = []
        stb_prev = {}
        wo_last = None
        qf_readers = []
        qr_readers = []
        ao_norm_toks = []
        rcp_free = None
        prev_store = []
        sts = [(b, st) for b in range(B) for st in range(NST)]

        def emit_attn_loads(b, st, semn="dXc"):
            par = st % 2
            W0 = 512 * st
            xs0 = 512 * NCH * par
            pb.waits("S", xs_cons[par])
            lx = []
            for c in range(NCH):
                if c < 4 or W0 > 0:
                    t0r = W0 if c < 4 else W0 - 1
                    lx.append(pb.op("S", lambda E, b=b, c=c, t0r=t0r, xs0=xs0: E.dma_start(
                        out=a_xs[:, xs0 + 512 * c:xs0 + 512 * (c + 1)],
                        in_=srcb[b, 128 * c:128 * (c + 1), t0r:t0r + 512]), inc=semn, n=16))
                else:
                    lx.append(pb.op("S", lambda E, b=b, c=c, xs0=xs0: E.dma_start(
                        out=a_xs[:, xs0 + 512 * c + 1:xs0 + 512 * (c + 1)],
                        in_=srcb[b, 128 * c:128 * (c + 1), 0:511]), inc=semn, n=16))
                    pb.waits("V", xs_cons[par])
                    tm0 = pb.op("V", lambda E, c=c, xs0=xs0: E.memset(a_xs[:, xs0 + 512 * c:xs0 + 512 * c + 1], 0.0), inc="eV")
                    lx.append(tm0)
            return lx

        pend_lx = {0: emit_attn_loads(*sts[0], semn="dXe")}
        for sti, (b, st) in enumerate(sts):
            if True:
                W0 = 512 * st
                par = st % 2
                k0 = 2 if st == 0 else 0
                pcs = pieces_for(k0)
                xs0 = 512 * NCH * par
                if sti + 1 < len(sts):
                    pend_lx[sti + 1] = emit_attn_loads(*sts[sti + 1], semn=("dXe" if (sti + 1) % 2 == 0 else "dXo"))
                lx = pend_lx.pop(sti)
                pb.waits("Y", tab_cons[par])
                ltabs = []
                for ti in range(4):
                    ltabs.append(pb.op("Y", lambda E, par=par, W0=W0, ti=ti: E.dma_start(
                        out=a_tab[:, 2048 * par + 512 * ti:2048 * par + 512 * (ti + 1)],
                        in_=tabs[ti, :, W0:W0 + 512]), inc="dTb", n=16))
                ltab = ltabs[-1]

                def tabap(ti, par=par):
                    return a_tab[:, 2048 * par + 512 * ti:2048 * par + 512 * (ti + 1)]

                pb.waits("T", lx + wl)
                pb.waits("G", stats_prev)
                qev = []
                sq_toks = []
                xs_cons[par] = []
                for c in range(NCH):
                    for (iw, coff, bk, evdst, scal) in (
                            (0, 0, (2 * c) % 4, a_qr, qs8_pp[:, l:l + 1]),
                            (1, 1024, (2 * c + 1) % 4, a_kr, ksc_pp[:, l:l + 1])):
                        pb.waits("T", qr_readers)
                        for ci in range(NCH):
                            tk = pb.op("T", lambda E, c=c, ci=ci, coff=coff, bk=bk, xs0=xs0: E.matmul(
                                bank(bk), a_wqkv[:, 3072 * ci + coff + 128 * c:3072 * ci + coff + 128 * (c + 1)],
                                a_xs[:, xs0 + 512 * ci:xs0 + 512 * (ci + 1)],
                                start=(ci == 0), stop=(ci == NCH - 1)),
                                inc=("eT" if ci == NCH - 1 else None))
                        xs_cons[par].append(tk)
                        pb.waits("V", [tk])
                        tev = pb.op("V", lambda E, c=c, bk=bk, evdst=evdst, scal=scal: E.tensor_scalar(
                            evdst[:, 512 * c:512 * (c + 1)], bank(bk), scal, None, ALU.mult), inc="eV")
                        qev.append(tev)
                        pb.waits("G", [tev])
                        tsq = pb.op("G", lambda E, c=c, iw=iw, evdst=evdst: E.tensor_mul(
                            a_q2[:, 512 * (8 * iw + c):512 * (8 * iw + c + 1)],
                            evdst[:, 512 * c:512 * (c + 1)],
                            evdst[:, 512 * c:512 * (c + 1)]), inc="eG")
                        sq_toks.append(tsq)
                        if len(qev) >= 4:
                            pb.waits("T", [qev[-4]])
                qr_readers = []
                vev = []
                for sub in range(4):
                    slot = ((W0 + 128 * sub) // 128) % 6
                    pb.waits("V", kv_readers.get((b, slot), [])[-2:])
                    for (ps, voff) in ((psA, 2048), (psB, 2560)):
                        for ci in range(NCH):
                            tk = pb.op("T", lambda E, sub=sub, ci=ci, voff=voff, ps=ps, xs0=xs0: E.matmul(
                                ps[:, :], a_xs[:, xs0 + 512 * ci + 128 * sub:xs0 + 512 * ci + 128 * (sub + 1)],
                                a_wqkv[:, 3072 * ci + voff:3072 * ci + voff + 512],
                                start=(ci == 0), stop=(ci == NCH - 1)),
                                inc=("eT" if ci == NCH - 1 else None))
                        xs_cons[par].append(tk)
                        pb.waits("V", [tk] + tz)
                        for pp in range(4):
                            p = pp + (0 if voff == 2048 else 4)
                            tv = pb.op("V", lambda E, slot=slot, p=p, pp=pp, ps=ps: E.tensor_copy(
                                a_vroll[:, 1280 * slot + 160 * p:1280 * slot + 160 * p + 64],
                                ps[:, 128 * pp:128 * pp + 64]), inc="eV")
                            tv2 = pb.op("V", lambda E, slot=slot, p=p, pp=pp, ps=ps: E.tensor_copy(
                                a_vroll[:, 1280 * slot + 160 * p + 96:1280 * slot + 160 * p + 160],
                                ps[:, 128 * pp + 64:128 * (pp + 1)]), inc="eV")
                            vev += [tv, tv2]
                        pb.waits("T", [vev[-1]])
                t_sq = None
                rn_casts = {}
                stats_cur = []
                for (iw, stat_bk, rnf, rnb) in ((0, 4, a_rnq, a_rnqb), (1, 5, a_rnk, a_rnkb)):
                    pb.waits("T", sq_toks + hsc_toks)
                    for c in range(NCH):
                        t_sq = pb.op("T", lambda E, c=c, iw=iw, stat_bk=stat_bk: E.matmul(
                            bank(stat_bk)[0:16, :], a_hsc[:, 128 * iw + 16 * c:128 * iw + 16 * (c + 1)],
                            a_q2[:, 512 * (8 * iw + c):512 * (8 * iw + c + 1)],
                            start=(c == 0), stop=(c == NCH - 1)), inc="eT")
                    stats_cur.append(t_sq)
                    pb.waits("V", [t_sq])
                    tr1 = pb.op("V", lambda E, rnf=rnf, stat_bk=stat_bk: E.tensor_scalar_max(
                        rnf[:, :], bank(stat_bk)[0:16, :], 1e-24), inc="eV")
                    pb.waits("S", [tr1])
                    tr3 = pb.op("S", lambda E, rnf=rnf: E.activation(rnf[:, :], rnf[:, :], AF.Ln), inc="eS")
                    pb.wait("S", tr3)
                    tr5 = pb.op("S", lambda E, rnf=rnf: E.activation(rnf[:, :], rnf[:, :], AF.Exp, scale=-0.5), inc="eS")
                    pb.waits("V", [tr5])
                    pb.waits("V", tz)
                    tr7 = pb.op("V", lambda E, rnf=rnf, rnb=rnb: E.tensor_copy(rnb[0:16, :], rnf[:, :]), inc="eV")
                    rn_casts[iw] = (tr1, tr7)
                rot_done = []
                pb.waits("T", vev)
                pb.waits("T", qf_readers)
                pb.waits("G", qev)
                items = []
                for c in range(NCH):
                    items.append((c, a_qr, a_rnqb, 0, 1, "qf", rn_casts[0]))
                    items.append((c, a_kr, a_rnkb, 2, 3, "kr", rn_casts[1]))
                slots4 = [((W0 // 128) + s) % 6 for s in range(4)]
                kr_write_gate = [kv_readers.get((b, s), [])[-1:] for s in slots4]
                prev_exp_ev = {}
                prev_rotmm = {}
                rnsb_free = {}
                for it, (c, evsrc, rnb, c_t, s_t, dstsel, casts) in enumerate(items):
                    ip = it % 2
                    xbk = 4 + ip
                    pb.waits("T", [casts[1], prev_exp_ev.get(ip)])
                    tx = pb.op("T", lambda E, c=c, rnb=rnb, xbk=xbk: E.matmul(
                        bank(xbk), ind16[:, 128 * c:128 * (c + 1)], rnb[0:16, :],
                        start=True, stop=True), inc="eT")
                    pb.waits("S", [tx, casts[1], rnsb_free.get(ip), stb_prev.get(ip)])
                    txe = pb.op("S", lambda E, ip=ip, xbk=xbk: E.activation(
                        a_rnsb[:, 512 * ip:512 * (ip + 1)], bank(xbk), AF.Copy), inc="eS")
                    prev_exp_ev[ip] = txe
                    tab_c = tabap(c_t)
                    tab_s = tabap(s_t)
                    pb.waits("G", [ltab, prev_rotmm.get(ip)] + qev)
                    tw1_ = pb.op("G", lambda E, c=c, evsrc=evsrc, tab_c=tab_c, ip=ip: E.tensor_tensor(
                        a_w1b[:, 512 * ip:512 * (ip + 1)], evsrc[:, 512 * c:512 * (c + 1)], tab_c, ALU.mult), inc="eG")
                    tw2_ = pb.op("G", lambda E, c=c, evsrc=evsrc, tab_s=tab_s, ip=ip: E.tensor_tensor(
                        a_w2b[:, 512 * ip:512 * (ip + 1)], evsrc[:, 512 * c:512 * (c + 1)], tab_s, ALU.mult), inc="eG")
                    tab_cons[par] = [tw2_]
                    qr_readers.append(tw2_)
                    rps = psA if ip == 0 else psB
                    pb.waits("T", [tw1_, tw2_])
                    pb.op("T", lambda E, ip=ip, rps=rps: E.matmul(rps[:, :], ident_bf[:, :], a_w1b[:, 512 * ip:512 * (ip + 1)], start=True, stop=False))
                    trr = pb.op("T", lambda E, ip=ip, rps=rps: E.matmul(rps[:, :], r2_bf[:, :], a_w2b[:, 512 * ip:512 * (ip + 1)], start=False, stop=True), inc="eT")
                    prev_rotmm[ip] = trr
                    pb.waits("V", [trr, txe])
                    if dstsel == "qf":
                        tev = pb.op("V", lambda E, c=c, rps=rps, ip=ip: E.tensor_tensor(
                            a_qf[:, 512 * c:512 * (c + 1)], rps[:, :],
                            a_rnsb[:, 512 * ip:512 * (ip + 1)], ALU.mult), inc="eV")
                    else:
                        pb.waits("V", kr_write_gate[0] + kr_write_gate[1] + kr_write_gate[2] + kr_write_gate[3])
                        runs = []
                        run = [slots4[0], 0, 1]
                        for s in range(1, 4):
                            if slots4[s] == run[0] + run[2]:
                                run[2] += 1
                            else:
                                runs.append(tuple(run))
                                run = [slots4[s], s, 1]
                        runs.append(tuple(run))
                        for (sl0, s0, ln_) in runs:
                            tev = pb.op("V", lambda E, c=c, sl0=sl0, s0=s0, ln_=ln_, rps=rps, ip=ip: E.tensor_tensor(
                                a_kroll[:, 768 * c + 128 * sl0:768 * c + 128 * (sl0 + ln_)],
                                rps[:, 128 * s0:128 * (s0 + ln_)],
                                a_rnsb[:, 512 * ip + 128 * s0:512 * ip + 128 * (s0 + ln_)], ALU.mult), inc="eV")
                    rot_done.append(tev)
                    prev_rotmm[ip] = tev
                    rnsb_free[ip] = tev
                heads = [(c, h) for c in range(NCH) for h in (0, 1)]
                qf_readers = []
                prev_hd = None
                pend_tail = None
                ao_norm_toks = []
                prev_exp_tok = None
                prev_mask = {0: None, 1: None}
                for idx, (c, h01) in enumerate(heads):
                    epar = idx % 2
                    pvps = psA if h01 == 0 else psB
                    pb.waits("T", rot_done + [prev_exp_tok])
                    sim_toks = []
                    for (k, jbr, s_rel, w) in pcs:
                        slot = ((W0 + jbr) // 128) % 6
                        sr = max(0, 128 * (k - 2))
                        en = min(512, 128 * (k + 1))
                        tk = pb.op("T", lambda E, c=c, h01=h01, k=k, slot=slot, sr=sr, en=en: E.matmul(
                            ps6[:, 512 * k + sr:512 * k + en],
                            a_kroll[64 * h01:64 * h01 + 64, 768 * c + 128 * slot:768 * c + 128 * (slot + 1)],
                            a_qf[64 * h01:64 * h01 + 64, 512 * c + sr:512 * c + en],
                            start=True, stop=True), inc="eT")
                        sim_toks.append(tk)
                        kv_readers.setdefault((b, slot), []).append(tk)
                        qf_readers.append(tk)
                    if pend_tail is not None:
                        pend_tail()
                        pend_tail = None
                    if prev_hd is not None:
                        pb.waits("T", [prev_hd["mask"]] + ao_norm_toks[-2:])
                        npc = len(pcs)
                        pvtok = None
                        for i2, (k, jbr, s_rel, w) in enumerate(pcs):
                            slot = ((W0 + jbr) // 128) % 6
                            aoff = 0 if prev_hd["h01"] == 0 else 32
                            pvtok = pb.op("T", lambda E, c2=prev_hd["c"], k=k, slot=slot, aoff=aoff,
                                          pv=prev_hd["pvps"], ep=prev_hd["epar"], i2=i2, npc=npc: E.matmul(
                                pv[:, :],
                                a_vroll[:, 1280 * slot + 160 * c2 + aoff:1280 * slot + 160 * c2 + aoff + 128],
                                a_e[:, 3072 * ep + 512 * k:3072 * ep + 512 * (k + 1)],
                                start=(i2 == 0), stop=(i2 == npc - 1)), inc="eT")
                            kv_readers.setdefault((b, slot), []).append(pvtok)
                        e_pv[prev_hd["epar"]] = pvtok
                        if prev_hd["h01"] == 1:
                            cc = prev_hd["c"]
                            pb.waits("S", [pvtok])
                            ts1_ = pb.op("S", lambda E: E.activation(a_rs[64:65, :], psA[64:65, :], AF.Ln), inc="eS")
                            ts2_ = pb.op("S", lambda E: E.activation(a_rs[32:33, :], psB[32:33, :], AF.Ln), inc="eS")
                            pb.waits("S", [ts1_, ts2_])
                            ts3_ = pb.op("S", lambda E: E.activation(a_rsb[64:65, :], a_rs[64:65, :], AF.Exp, scale=-1.0), inc="eS")
                            ts4_ = pb.op("S", lambda E: E.activation(a_rsb[32:33, :], a_rs[32:33, :], AF.Exp, scale=-1.0), inc="eS")
                            pb.waits("V", [pvtok, wo_last])
                            te1 = pb.op("V", lambda E, cc=cc: E.tensor_copy(a_ao[0:64, 512 * cc:512 * (cc + 1)], psA[0:64, :]), inc="eV")
                            te2 = pb.op("V", lambda E, cc=cc: E.tensor_copy(a_ao[64:128, 512 * cc:512 * (cc + 1)], psB[64:128, :]), inc="eV")

                            def mk_tail(cc=cc, ts3_=ts3_, ts4_=ts4_, te1=te1, te2=te2):
                                def tail():
                                    pb.waits("T", [ts3_, ts4_, te1, te2])
                                    tb1 = pb.op("T", lambda E: E.matmul(psA[:, :], onesrow64[64:65, :], a_rsb[64:65, :], start=True, stop=True), inc="eT")
                                    tb2 = pb.op("T", lambda E: E.matmul(psB[:, :], onesrow64[32:33, :], a_rsb[32:33, :], start=True, stop=True), inc="eT")
                                    pb.waits("V", [tb1, tb2])
                                    tn1 = pb.op("V", lambda E, cc=cc: E.tensor_tensor(
                                        a_ao[0:64, 512 * cc:512 * (cc + 1)], a_ao[0:64, 512 * cc:512 * (cc + 1)],
                                        psA[0:64, :], ALU.mult), inc="eV")
                                    tn2 = pb.op("V", lambda E, cc=cc: E.tensor_tensor(
                                        a_ao[64:128, 512 * cc:512 * (cc + 1)], a_ao[64:128, 512 * cc:512 * (cc + 1)],
                                        psB[64:128, :], ALU.mult), inc="eV")
                                    ao_norm_toks.extend([tn1, tn2])
                                    return [tn1, tn2]
                                return tail
                            pend_tail = mk_tail()
                    pb.waits("S", sim_toks + [e_pv[epar]])
                    exp_toks = []
                    for (k, jbr, s_rel, w) in pcs:
                        sr = max(0, 128 * (k - 2))
                        en = min(512, 128 * (k + 1))
                        exp_toks.append(pb.op("S", lambda E, k=k, sr=sr, en=en, epar=epar, l=l: E.activation(
                            a_e[:, 3072 * epar + 512 * k + sr:3072 * epar + 512 * k + en],
                            ps6[:, 512 * k + sr:512 * k + en], AF.Exp, bias=negC_pp[:, l:l + 1]), inc="eS"))
                    prev_exp_tok = exp_toks[-1]
                    pb.waits("V", exp_toks + [prev_mask[epar]])
                    t_mk = pb.op("V", lambda E, k0=k0, epar=epar: E.tensor_tensor(
                        a_e[:, 3072 * epar + 512 * k0:3072 * epar + 3072],
                        a_e[:, 3072 * epar + 512 * k0:3072 * epar + 3072],
                        a_mask[:, 512 * k0:3072], ALU.mult), inc="eV")
                    prev_mask[epar] = t_mk
                    prev_hd = {"c": c, "h01": h01, "epar": epar, "pvps": pvps, "mask": t_mk}
                pb.waits("T", [prev_hd["mask"]] + ao_norm_toks[-2:])
                npc = len(pcs)
                pvtok = None
                for i2, (k, jbr, s_rel, w) in enumerate(pcs):
                    slot = ((W0 + jbr) // 128) % 6
                    pvtok = pb.op("T", lambda E, c2=prev_hd["c"], k=k, slot=slot,
                                  ep=prev_hd["epar"], i2=i2, npc=npc: E.matmul(
                        psB[:, :],
                        a_vroll[:, 1280 * slot + 160 * c2 + 32:1280 * slot + 160 * c2 + 160],
                        a_e[:, 3072 * ep + 512 * k:3072 * ep + 512 * (k + 1)],
                        start=(i2 == 0), stop=(i2 == npc - 1)), inc="eT")
                    kv_readers.setdefault((b, slot), []).append(pvtok)
                e_pv[prev_hd["epar"]] = pvtok
                if pend_tail is not None:
                    pend_tail()
                    pend_tail = None
                cc = prev_hd["c"]
                pb.waits("S", [pvtok])
                ts1_ = pb.op("S", lambda E: E.activation(a_rs[64:65, :], psA[64:65, :], AF.Ln), inc="eS")
                ts2_ = pb.op("S", lambda E: E.activation(a_rs[32:33, :], psB[32:33, :], AF.Ln), inc="eS")
                pb.waits("S", [ts1_, ts2_])
                ts3_ = pb.op("S", lambda E: E.activation(a_rsb[64:65, :], a_rs[64:65, :], AF.Exp, scale=-1.0), inc="eS")
                ts4_ = pb.op("S", lambda E: E.activation(a_rsb[32:33, :], a_rs[32:33, :], AF.Exp, scale=-1.0), inc="eS")
                pb.waits("V", [pvtok, wo_last])
                te1 = pb.op("V", lambda E, cc=cc: E.tensor_copy(a_ao[0:64, 512 * cc:512 * (cc + 1)], psA[0:64, :]), inc="eV")
                te2 = pb.op("V", lambda E, cc=cc: E.tensor_copy(a_ao[64:128, 512 * cc:512 * (cc + 1)], psB[64:128, :]), inc="eV")
                pb.waits("T", [ts3_, ts4_, te1, te2])
                tb1 = pb.op("T", lambda E: E.matmul(psA[:, :], onesrow64[64:65, :], a_rsb[64:65, :], start=True, stop=True), inc="eT")
                tb2 = pb.op("T", lambda E: E.matmul(psB[:, :], onesrow64[32:33, :], a_rsb[32:33, :], start=True, stop=True), inc="eT")
                pb.waits("V", [tb1, tb2])
                tn1 = pb.op("V", lambda E, cc=cc: E.tensor_tensor(
                    a_ao[0:64, 512 * cc:512 * (cc + 1)], a_ao[0:64, 512 * cc:512 * (cc + 1)],
                    psA[0:64, :], ALU.mult), inc="eV")
                tn2 = pb.op("V", lambda E, cc=cc: E.tensor_tensor(
                    a_ao[64:128, 512 * cc:512 * (cc + 1)], a_ao[64:128, 512 * cc:512 * (cc + 1)],
                    psB[64:128, :], ALU.mult), inc="eV")
                ao_norm_toks.extend([tn1, tn2])
                st_tile = []
                lxr = {}
                st_prev = {}
                pb.waits("Y", prev_store)
                lxr[0] = pb.op("Y", lambda E, b=b, W0=W0: E.dma_start(
                    out=a_xres[:, 0:512], in_=src[b, 0:128, W0:W0 + 512]), inc="dXr", n=16)
                pb.waits("T", ao_norm_toks + [tn1, tn2])
                for c in range(NCH):
                    if c + 1 < NCH:
                        lxr[(c + 1) % 2] = pb.op("Y", lambda E, b=b, c=c, W0=W0: E.dma_start(
                            out=a_xres[:, 512 * ((c + 1) % 2):512 * ((c + 1) % 2) + 512],
                            in_=src[b, 128 * (c + 1):128 * (c + 2), W0:W0 + 512]), inc="dXr", n=16)
                    bk = 2 + c % 2
                    for ci in range(NCH):
                        tk = pb.op("T", lambda E, c=c, ci=ci, bk=bk: E.matmul(
                            bank(bk), a_wo[:, 1024 * ci + 128 * c:1024 * ci + 128 * (c + 1)],
                            a_ao[:, 512 * ci:512 * (ci + 1)],
                            start=(ci == 0), stop=(ci == NCH - 1)),
                            inc=("eT" if ci == NCH - 1 else None))
                    wo_last = tk
                    pb.waits("V", [tk, lxr[c % 2], st_prev.get(c % 2)])
                    te = pb.op("V", lambda E, c=c, bk=bk: E.scalar_tensor_tensor(
                        a_xnew[:, 512 * (c % 2):512 * (c % 2) + 512], bank(bk), 1.0,
                        a_xres[:, 512 * (c % 2):512 * (c % 2) + 512], ALU.mult, ALU.add), inc="eV")
                    pb.waits("V", [stb_prev.get(c % 2), te, prev_exp_ev.get(c % 2)])
                    tcb = pb.op("V", lambda E, c=c: E.tensor_copy(
                        a_xnb[:, 512 * (c % 2):512 * (c % 2) + 512],
                        a_xnew[:, 512 * (c % 2):512 * (c % 2) + 512]), inc="eV")
                    pb.waits("T", [te])
                    pb.waits("Y", [te, tcb])
                    ts_ = pb.op("Y", lambda E, b=b, c=c, W0=W0: E.dma_start(
                        out=dst[b, 128 * c:128 * (c + 1), W0:W0 + 512],
                        in_=a_xnew[:, 512 * (c % 2):512 * (c % 2) + 512]), inc="dSt", n=16)
                    tsb_ = pb.op("Y", lambda E, b=b, c=c, W0=W0: E.dma_start(
                        out=dstb[b, 128 * c:128 * (c + 1), W0:W0 + 512],
                        in_=a_xnb[:, 512 * (c % 2):512 * (c % 2) + 512]), inc="dSt", n=16)
                    st_tile.append(ts_)
                    st_tile.append(tsb_)
                    store_toks.append(ts_)
                    st_prev[c % 2] = ts_
                    stb_prev[c % 2] = tsb_
                prev_store = st_tile
                stats_prev = stats_cur
        return store_toks


    def emit_ffn_stage(l, src, srcb, dst, dstb):
        f_w1 = at("f_w1", [128, NCH * D2], BF16, 0)
        f_w2 = at("f_w2", [128, 22 * 1024], BF16, 87360)
        f_xs = at("f_xs", [128, 2 * NCH * 512], BF16, 132416)
        f_xn = at("f_xn", [128, 2 * NCH * 512], BF16, 148800)
        f_h = at("f_h", [128, 22 * 512], BF16, 165184)
        f_sq = at("f_sq", [128, 512], BF16, 187712)
        f_ms = at("f_ms", [1, 3 * 512], BF16, 188736)
        f_rstd = at("f_rstd", [1, 512], BF16, 191808)
        f_cbr = at("f_cbr", [2, 512], BF16, 192832)
        f_gb = at("f_gb", [2, C], BF16, 193856)
        f_gpp = at("f_gpp", [128, NCH], F32, 195904)
        f_bpp = at("f_bpp", [128, NCH], F32, 195968)
        f_b1a = at("f_b1a", [128, 22], F32, 196032)
        f_b1g = at("f_b1g", [128, 22], F32, 196160)
        f_b2 = at("f_b2", [128, NCH], F32, 196288)
        f_gt = at("f_gt", [128, 2 * 512], BF16, 196352)
        f_xres = at("f_xres", [128, 2 * 512], F32, 198400)
        f_xnew = at("f_xnew", [128, 2 * 512], F32, 202496)
        f_xnb = at("f_xnb", [128, 2 * 512], BF16, 196352)
        prev = stage_barrier_toks()
        pb.waits("G", prev)
        pb.waits("Y", prev)
        pb.waits("S", prev)
        wl = []
        wl_small = []
        for ci in range(NCH):
            wl.append(pb.op("G", lambda E, ci=ci: E.dma_start(
                out=f_w1[:, D2 * ci:D2 * (ci + 1)], in_=w1[l, 128 * ci:128 * (ci + 1), :]), inc="dW", n=16))
        for j in range(22):
            wj = min(128, DI - 128 * j)
            wl.append(pb.op("G", lambda E, j=j, wj=wj: E.dma_start(
                out=f_w2[0:wj, 1024 * j:1024 * (j + 1)], in_=w2[l, 128 * j:128 * j + wj, :]), inc="dW", n=16))
            wl_small.append(pb.op("G", lambda E, j=j, wj=wj, l=l: E.dma_start(
                out=f_b1a[0:wj, j:j + 1], in_=b1_in[l, 128 * j:128 * j + wj]), inc="dWs", n=16))
            wl_small.append(pb.op("G", lambda E, j=j, wj=wj, l=l: E.dma_start(
                out=f_b1g[0:wj, j:j + 1], in_=b1_in[l, DI + 128 * j:DI + 128 * j + wj]), inc="dWs", n=16))
        for c in range(NCH):
            wl_small.append(pb.op("G", lambda E, c=c, l=l: E.dma_start(
                out=f_gpp[:, c:c + 1], in_=lng_in[l, 128 * c:128 * (c + 1)]), inc="dWs", n=16))
            wl_small.append(pb.op("G", lambda E, c=c, l=l: E.dma_start(
                out=f_bpp[:, c:c + 1], in_=lnb_in[l, 128 * c:128 * (c + 1)]), inc="dWs", n=16))
            wl_small.append(pb.op("G", lambda E, c=c, l=l: E.dma_start(
                out=f_b2[:, c:c + 1], in_=b2_in[l, 128 * c:128 * (c + 1)]), inc="dWs", n=16))
        wl_small.append(pb.op("G", lambda E, l=l: E.dma_start(out=f_gb[0:1, :], in_=lng_in[l, None, :]), inc="dWs", n=16))
        wl_small.append(pb.op("G", lambda E, l=l: E.dma_start(out=f_gb[1:2, :], in_=lnb_in[l, None, :]), inc="dWs", n=16))
        pb.waits("Y", [tones512])
        wl_small.append(pb.op("Y", lambda E: E.dma_start(out=f_cbr[1:2, :], in_=ones512_b[0:1, :]), inc="dO1", n=16))

        store_toks = []
        xs_cons = {0: [], 1: []}
        xn_cons = {0: [], 1: []}
        prev_store = []
        prev_ctx = None
        h_free = None
        gt_free = {0: None, 1: None}
        tg_last = {}
        f_stb_prev = {}
        f_psum_prev = []

        def emit_w1w2(ctx):
            nonlocal h_free, prev_store
            b2_, st2, par2, xn_toks = ctx
            W0 = 512 * st2
            xn0 = 512 * NCH * par2
            h_toks = []
            pb.waits("T", xn_toks + wl + wl_small)
            for j in range(22):
                wj = min(128, DI - 128 * j)
                abk = j % 3
                gbk = 3 + j % 3
                for ci in range(NCH):
                    tka = pb.op("T", lambda E, j=j, ci=ci, abk=abk, wj=wj, xn0=xn0: E.matmul(
                        bank(abk)[0:wj, :],
                        f_w1[:, D2 * ci + 128 * j:D2 * ci + 128 * j + wj],
                        f_xn[:, xn0 + 512 * ci:xn0 + 512 * (ci + 1)],
                        start=(ci == 0), stop=(ci == NCH - 1)),
                        inc=("eT" if ci == NCH - 1 else None))
                for ci in range(NCH):
                    tkg = pb.op("T", lambda E, j=j, ci=ci, gbk=gbk, wj=wj, xn0=xn0: E.matmul(
                        bank(gbk)[0:wj, :],
                        f_w1[:, D2 * ci + DI + 128 * j:D2 * ci + DI + 128 * j + wj],
                        f_xn[:, xn0 + 512 * ci:xn0 + 512 * (ci + 1)],
                        start=(ci == 0), stop=(ci == NCH - 1)),
                        inc=("eT" if ci == NCH - 1 else None))
                xn_cons[par2] += [tka, tkg]
                gp = j % 2
                pb.waits("S", [tkg, gt_free[gp], f_stb_prev.get(gp)])
                tg = pb.op("S", lambda E, j=j, gbk=gbk, wj=wj, gp=gp: E.activation(
                    f_gt[0:wj, 512 * gp:512 * gp + 512], bank(gbk)[0:wj, :], AF.Gelu, bias=f_b1g[0:wj, j:j + 1]), inc="eS")
                tg_last[gp] = tg
                pb.waits("V", [tka, tg, h_free])
                th = pb.op("V", lambda E, j=j, abk=abk, wj=wj, gp=gp: E.scalar_tensor_tensor(
                    f_h[0:wj, 512 * j:512 * (j + 1)], bank(abk)[0:wj, :],
                    f_b1a[0:wj, j:j + 1], f_gt[0:wj, 512 * gp:512 * gp + 512], ALU.add, ALU.mult), inc="eV")
                h_toks.append(th)
                gt_free[gp] = th
                if len(h_toks) >= 3:
                    pb.waits("T", [h_toks[-3]])
            pb.waits("T", h_toks)
            lxr = {}
            st_prev = {}
            pb.waits("Y", prev_store)
            lxr[0] = pb.op("Y", lambda E, b2_=b2_, W0=W0: E.dma_start(
                out=f_xres[:, 0:512], in_=src[b2_, 0:128, W0:W0 + 512]), inc="dXr", n=16)
            st_tile = []
            wlast = None
            for c in range(NCH):
                if c + 1 < NCH:
                    lxr[(c + 1) % 2] = pb.op("Y", lambda E, b2_=b2_, c=c, W0=W0: E.dma_start(
                        out=f_xres[:, 512 * ((c + 1) % 2):512 * ((c + 1) % 2) + 512],
                        in_=src[b2_, 128 * (c + 1):128 * (c + 2), W0:W0 + 512]), inc="dXr", n=16)
                bk = c % 2
                for j in range(22):
                    wj = min(128, DI - 128 * j)
                    tk = pb.op("T", lambda E, c=c, j=j, bk=bk, wj=wj: E.matmul(
                        bank(bk), f_w2[0:wj, 1024 * j + 128 * c:1024 * j + 128 * (c + 1)],
                        f_h[0:wj, 512 * j:512 * (j + 1)],
                        start=(j == 0), stop=(j == 21)),
                        inc=("eT" if j == 21 else None))
                wlast = tk
                pb.waits("V", [tk, lxr[c % 2], st_prev.get(c % 2)])
                te = pb.op("V", lambda E, c=c, bk=bk: E.scalar_tensor_tensor(
                    f_xnew[:, 512 * (c % 2):512 * (c % 2) + 512], bank(bk),
                    f_b2[:, c:c + 1], f_xres[:, 512 * (c % 2):512 * (c % 2) + 512],
                    ALU.add, ALU.add), inc="eV")
                pb.waits("V", [f_stb_prev.get(c % 2), te, tg_last.get(c % 2)])
                tcb = pb.op("V", lambda E, c=c: E.tensor_copy(
                    f_xnb[:, 512 * (c % 2):512 * (c % 2) + 512],
                    f_xnew[:, 512 * (c % 2):512 * (c % 2) + 512]), inc="eV")
                pb.waits("T", [te])
                pb.waits("Y", [te, tcb])
                ts_ = pb.op("Y", lambda E, b2_=b2_, c=c, W0=W0: E.dma_start(
                    out=dst[b2_, 128 * c:128 * (c + 1), W0:W0 + 512],
                    in_=f_xnew[:, 512 * (c % 2):512 * (c % 2) + 512]), inc="dSt", n=16)
                tsb_ = pb.op("Y", lambda E, b2_=b2_, c=c, W0=W0: E.dma_start(
                    out=dstb[b2_, 128 * c:128 * (c + 1), W0:W0 + 512],
                    in_=f_xnb[:, 512 * (c % 2):512 * (c % 2) + 512]), inc="dSt", n=16)
                st_tile.append(ts_)
                st_tile.append(tsb_)
                store_toks.append(ts_)
                st_prev[c % 2] = ts_
                f_stb_prev[c % 2] = tsb_
            h_free = wlast
            prev_store = st_tile

        sts = [(b, st) for b in range(B) for st in range(NST)]

        def emit_loads(bi, sti, semn="dXc"):
            par = sti % 2
            xs0 = 512 * NCH * par
            W0 = 512 * sti
            pb.waits("S", xs_cons[par])
            lx = []
            for c in range(NCH):
                if c < 4 or W0 > 0:
                    t0r = W0 if c < 4 else W0 - 1
                    lx.append(pb.op("S", lambda E, bi=bi, c=c, t0r=t0r, xs0=xs0: E.dma_start(
                        out=f_xs[:, xs0 + 512 * c:xs0 + 512 * (c + 1)],
                        in_=srcb[bi, 128 * c:128 * (c + 1), t0r:t0r + 512]), inc=semn, n=16))
                else:
                    lx.append(pb.op("S", lambda E, bi=bi, c=c, xs0=xs0: E.dma_start(
                        out=f_xs[:, xs0 + 512 * c + 1:xs0 + 512 * (c + 1)],
                        in_=srcb[bi, 128 * c:128 * (c + 1), 0:511]), inc=semn, n=16))
                    pb.waits("V", xs_cons[par])
                    tm0 = pb.op("V", lambda E, c=c, xs0=xs0: E.memset(f_xs[:, xs0 + 512 * c:xs0 + 512 * c + 1], 0.0), inc="eV")
                    lx.append(tm0)
            return lx

        pend_loads = {}
        pend_loads[0] = emit_loads(*sts[0], semn="dXe")
        for i, (b, st) in enumerate(sts):
            par = st % 2
            if i + 1 < len(sts):
                pend_loads[i + 1] = emit_loads(*sts[i + 1], semn=("dXe" if (i + 1) % 2 == 0 else "dXo"))
            lx = pend_loads.pop(i)
            pb.waits("V", lx + wl_small + xn_cons[par])
            pb.waits("S", lx)
            pb.waits("T", lx + wl_small)
            pb.waits("G", lx + xn_cons[par])
            xs0 = 512 * NCH * par
            xs_view = f_xs[:, xs0:xs0 + 512 * NCH]
            xn_view = f_xn[:, 512 * NCH * par:512 * NCH * par + 512 * NCH]
            toks_xn, _ = ln_stats_apply(
                xs_view, xs_view, xn_view, f_gb, f_gpp, lx + f_psum_prev, f_sq, f_ms,
                (f_rstd, f_cbr), width=512, bpp_ap=f_bpp)
            f_psum_prev = toks_xn
            xs_cons[par] = [pb.tok("eT"), pb.tok("eG"), pb.tok("eV")]
            xn_cons[par] = []
            if prev_ctx is not None:
                emit_w1w2(prev_ctx)
            prev_ctx = (b, st, par, toks_xn)
        emit_w1w2(prev_ctx)
        return store_toks

    cur = scrA if not skip_pre else x_in
    curb = scrAb
    for l in range(DEPTH):
        emit_attn_stage(l, cur, curb, scrP, scrPb)
        if tap == f"lay{l}a":
            return emit_tap(scrP)
        nxt, nxtb = (scrQ, scrQb) if l % 2 == 0 else (scr1, scr1b)
        emit_ffn_stage(l, scrP, scrPb, nxt, nxtb)
        if tap == f"lay{l}f":
            return emit_tap(nxt)
        cur, curb = nxt, nxtb

    fl_z = at("fl_z", [128, NCH * 512], F32, 0)
    fl_xa = at("fl_xa", [128, NCH * 512], F32, 16 * 1024)
    fl_s = at("fl_s", [128, NCH * 512], F32, 32 * 1024)
    fl_sb = at("fl_sb", [128, NCH * 512], BF16, 48 * 1024)
    fl_sq = at("fl_sq", [128, 512], BF16, 56 * 1024)
    fl_ms = at("fl_ms", [1, 4 * 512], F32, 57 * 1024)
    fl_o = at("fl_o", [128, NCH * 512], F32, 66 * 1024)
    prevf = stage_barrier_toks()
    pb.waits("Y", prevf)
    cons_prev = []
    final_st = []
    for b in range(B):
        for st in range(NST):
            W0 = 512 * st
            pb.waits("Y", cons_prev)
            lx = []
            for c in range(NCH):
                lx.append(pb.op("Y", lambda E, b=b, c=c, W0=W0: E.dma_start(
                    out=fl_z[:, 512 * c:512 * (c + 1)], in_=cur[b, 128 * c:128 * (c + 1), W0:W0 + 512]), inc="dX", n=16))
                lx.append(pb.op("Y", lambda E, b=b, c=c, W0=W0: E.dma_start(
                    out=fl_xa[:, 512 * c:512 * (c + 1)], in_=scrA[b, 128 * c:128 * (c + 1), W0:W0 + 512]), inc="dX", n=16))
            pb.waits("V", lx + cons_prev)
            sv = []
            for c in range(NCH):
                tv = pb.op("V", lambda E, c=c: E.tensor_add(
                    fl_s[:, 512 * c:512 * (c + 1)], fl_z[:, 512 * c:512 * (c + 1)],
                    fl_xa[:, 512 * c:512 * (c + 1)]), inc="eV")
                pb.wait("V", tv)
                tc_ = pb.op("V", lambda E, c=c: E.tensor_copy(
                    fl_sb[:, 512 * c:512 * (c + 1)], fl_s[:, 512 * c:512 * (c + 1)]), inc="eV")
                sv.append(tc_)
            pb.waits("S", sv)
            pb.waits("T", sv)
            toks_o, _ = ln_stats_apply(
                fl_s, fl_sb, None, None, None, sv, fl_sq, fl_ms, None,
                width=512, fp32_bcast=True, no_gb=True, outf32=fl_o)
            pb.waits("Y", toks_o)
            stt_ = []
            for c in range(NCH):
                ts_ = pb.op("Y", lambda E, b=b, c=c, W0=W0: E.dma_start(
                    out=out[b, 128 * c:128 * (c + 1), W0:W0 + 512],
                    in_=fl_o[:, 512 * c:512 * (c + 1)]), inc="dOut", n=16)
                stt_.append(ts_)
            final_st += stt_
            cons_prev = [pb.tok("eV"), pb.tok("eT"), pb.tok("eS")] + stt_
    pb.op("Y", lambda E: E.wait_ge(pb.sem_h["dOut"], pb.cnt["dOut"]))
    with nc.Block() as block:
        pb.finish(block)
    return nc


def _prep_inputs(inputs, core):
    i = inputs
    x = np.asarray(i["L0_attn_x"])[2 * core:2 * core + 2]
    m = {
        "x_fm": np.ascontiguousarray(np.transpose(x, (0, 2, 1))),
        "wkT": np.ascontiguousarray(np.asarray(i["L0_attn_Wk"]).T),
        "wvT": np.ascontiguousarray(np.asarray(i["L0_attn_Wv"]).T),
        "wrT": np.ascontiguousarray(np.asarray(i["L0_attn_Wr"]).T),
        "woT": np.ascontiguousarray(np.asarray(i["L0_attn_Wo"]).T),
        "wffkT": np.ascontiguousarray(np.asarray(i["L0_attn_Wffk"]).T),
        "wffvT": np.ascontiguousarray(np.asarray(i["L0_attn_Wffv"]).T),
        "ln1g": np.asarray(i["L0_attn_ln1_g"]),
        "ln1b": np.asarray(i["L0_attn_ln1_b"]),
        "ln2g": np.asarray(i["L0_attn_ln2_g"]),
        "ln2b": np.asarray(i["L0_attn_ln2_b"]),
        "td": np.ascontiguousarray(np.asarray(i["L0_attn_time_decay"]).reshape(-1)),
        "tf": np.ascontiguousarray(np.asarray(i["L0_attn_time_first"]).reshape(-1)),
        "wqkv": np.ascontiguousarray(np.asarray(i["L0_ffn_Wqkv"])),
        "wo2": np.ascontiguousarray(np.asarray(i["L0_ffn_Wo"])),
        "w1": np.ascontiguousarray(np.asarray(i["L0_ffn_W1"])),
        "w2": np.ascontiguousarray(np.asarray(i["L0_ffn_W2"])),
        "b1": np.ascontiguousarray(np.asarray(i["L0_ffn_b1"])),
        "b2": np.ascontiguousarray(np.asarray(i["L0_ffn_b2"])),
        "lng": np.ascontiguousarray(np.asarray(i["L0_ffn_ln_g"])),
        "lnb": np.ascontiguousarray(np.asarray(i["L0_ffn_ln_b"])),
        "qs": np.ascontiguousarray(np.asarray(i["L0_ffn_q_scale"])),
        "ks": np.ascontiguousarray(np.asarray(i["L0_ffn_k_scale"])),
    }
    return {k: np.asarray(v, dtype=np.float32) for k, v in m.items()}


def kernel(**inputs):
    nc = build(T=4096, tap=None)
    in_maps = [_prep_inputs(inputs, c) for c in range(8)]
    res = run_bass_kernel_spmd(nc, in_maps, core_ids=list(range(8)))
    shards = []
    for c in range(8):
        o = np.asarray(res.results[c]["out_fm"])
        shards.append(np.transpose(o, (0, 2, 1)))
    return np.ascontiguousarray(np.concatenate(shards, axis=0)).astype(np.float32)
